# version 02
# baseline (speedup 1.0000x reference)
import contextlib
import numpy as np
import concourse.bass as bass
import concourse.mybir as mybir
from concourse.bass_utils import run_bass_kernel_spmd

F32 = mybir.dt.float32
BF16 = mybir.dt.bfloat16
AF = mybir.ActivationFunctionType
ALU = mybir.AluOpType
AX = mybir.AxisListType.X

NCORES = 8
T = 1024
TB = 8
C = 2048
CB = 16
H = 16
MAGIC = 12582912.0
EPS_LN = 1e-5
EPS_RMS = 1e-6
EPS_SCALE = 1e-5
EPS_Z = 1e-6


class Prog:
    ENGINES = ("sync", "gpsimd", "tensor", "vector", "scalar")

    def __init__(self):
        self.ins = {e: [] for e in self.ENGINES}
        self.sem_ct = {}

    def emit(self, eng, fn):
        self.ins[eng].append(fn)

    def inc(self, sem, amount):
        self.sem_ct[sem] = self.sem_ct.get(sem, 0) + amount
        return self.sem_ct[sem]

    def cur(self, sem):
        return self.sem_ct.get(sem, 0)


def build_nc():
    nc = bass.Bass("TRN2", target_bir_lowering=False, debug=False)

    x_fm_d = nc.dram_tensor("x_fm", [C, T], F32, kind="ExternalInput").ap()
    x_tok_d = nc.dram_tensor("x_tok", [T, C], F32, kind="ExternalInput").ap()
    wq_d = nc.dram_tensor("wq_t", [C, C], F32, kind="ExternalInput").ap()
    wk_d = nc.dram_tensor("wk_t", [C, C], F32, kind="ExternalInput").ap()
    wv_d = nc.dram_tensor("wv_t", [C, C], F32, kind="ExternalInput").ap()
    wo_d = nc.dram_tensor("wo_t", [C, C], F32, kind="ExternalInput").ap()
    wf_d = nc.dram_tensor("wf_t", [C, C], F32, kind="ExternalInput").ap()
    gam_d = nc.dram_tensor("gamma_bc", [128, C], F32, kind="ExternalInput").ap()
    id_d = nc.dram_tensor("id_bf", [128, 128], F32, kind="ExternalInput").ap()
    ones_d = nc.dram_tensor("ones_f", [128, 128], F32, kind="ExternalInput").ap()
    y_d = nc.dram_tensor("y", [T, C], F32, kind="ExternalOutput").ap()

    cc1_in = nc.dram_tensor("cc1_in", [128, 2064], F32)
    cc1_out = nc.dram_tensor("cc1_out", [128, 2064], F32)
    cc2_in = nc.dram_tensor("cc2_in", [1, 128], F32)
    cc2_out = nc.dram_tensor("cc2_out", [1, 128], F32)

    P = Prog()
    es = contextlib.ExitStack()

    def sb(name, shape, dtype=F32):
        return es.enter_context(nc.sbuf_tensor(name, shape, dtype))

    RREG = sb("RREG", [128, TB * C])
    GAM = sb("GAM", [128, C])
    WFT = sb("WFT", [128, C])
    FSCR = sb("FSCR", [128, 1024])
    WST = sb("WST", [128, 1536])
    S1f = sb("S1f", [128, 8192])
    S2f = sb("S2f", [128, 8192])
    S3f = sb("S3f", [128, 8192])
    S1 = S1f.bitcast(BF16)
    S2 = S2f.bitcast(BF16)
    S3 = S3f.bitcast(BF16)
    WSTB = sb("WSTB", [128, 768]).bitcast(BF16)
    KVSB = sb("KVSB", [128, 1032]).bitcast(BF16)
    ZB = sb("ZB", [128, 256]).bitcast(BF16)
    IDB = sb("IDB", [128, 64]).bitcast(BF16)
    ONEB = sb("ONEB", [128, 64]).bitcast(BF16)
    ZBC = sb("ZBC", [128, T])
    ZTOK = sb("ZTOK", [128, 128])
    BNS = sb("BNS", [128, 24])
    STA = sb("STA", [128, 64])
    ACC = sb("ACC", [128, 32])
    COLS = sb("COLS", [128, 8])
    ROWS = sb("ROWS", [1, 512])
    ZR = sb("ZR", [1, 1024]).bitcast(BF16)
    ONESF = sb("ONESF", [1, 128])

    KVP = RREG[:, 1024:3088]

    PS = [es.enter_context(nc.psum_tensor(f"ps{i}", [128, 512], F32)) for i in range(6)]
    PT = [es.enter_context(nc.psum_tensor(f"pt{i}", [128, 1024], BF16)) for i in range(2)]
    PS.append(PT[0].bitcast(F32))
    PS.append(PT[1].bitcast(F32))

    sems = {}
    for s in ("dS", "dG", "dC", "tS", "vS", "sS", "cc"):
        sems[s] = es.enter_context(nc.semaphore(s))

    def dma(eng, qsem, out, in_):
        tgt = P.inc(qsem, 16)
        P.emit(eng, lambda e, o=out, i_=in_, s=sems[qsem]: e.dma_start(out=o, in_=i_).then_inc(s, 16))
        return tgt

    def wait(eng, sem, tgt):
        if tgt and tgt > 0:
            P.emit(eng, lambda e, s=sems[sem], t=tgt: e.wait_ge(s, t))

    def vop(fn, inc=True):
        if inc:
            tgt = P.inc("vS", 1)
            P.emit("vector", lambda e, f=fn, s=sems["vS"]: f(e).then_inc(s, 1))
            return tgt
        P.emit("vector", lambda e, f=fn: f(e))
        return P.cur("vS")

    def sop(fn, inc=True):
        if inc:
            tgt = P.inc("sS", 1)
            P.emit("scalar", lambda e, f=fn, s=sems["sS"]: f(e).then_inc(s, 1))
            return tgt
        P.emit("scalar", lambda e, f=fn: f(e))
        return P.cur("sS")

    def top(fn, inc=True):
        if inc:
            tgt = P.inc("tS", 1)
            P.emit("tensor", lambda e, f=fn, s=sems["tS"]: f(e).then_inc(s, 1))
            return tgt
        P.emit("tensor", lambda e, f=fn: f(e))
        return P.cur("tS")

    d_id = dma("gpsimd", "dG", IDB[:, 0:128], id_d)
    d_ones = dma("gpsimd", "dG", ONEB[:, 0:128], ones_d)
    d_onesf = dma("sync", "dS", ONESF[0:1, :], ones_d[0:1, :])
    d_gam = dma("sync", "dS", GAM[:], gam_d)
    d_xfm_last = 0
    for c in range(CB):
        d_xfm_last = dma("gpsimd", "dG", S1[:, c * T:(c + 1) * T], x_fm_d[c * 128:(c + 1) * 128, :])

    class WStream:
        def __init__(self):
            self.i = 0
            self.f32_free = [0, 0, 0]
            self.bf_free = [0, 0, 0]

        def load(self, src_ap):
            sl = self.i % 3
            self.i += 1
            f32_ap = WST[:, sl * 512:(sl + 1) * 512]
            bf_ap = WSTB[:, sl * 512:(sl + 1) * 512]
            wait("sync", "sS", self.f32_free[sl])
            d_tgt = dma("sync", "dS", f32_ap, src_ap)
            wait("scalar", "dS", d_tgt)
            wait("scalar", "tS", self.bf_free[sl])
            c_tgt = sop(lambda e, o=bf_ap, i_=f32_ap: e.copy(out=o, in_=i_))
            self.f32_free[sl] = c_tgt
            return bf_ap, c_tgt, sl

        def consumed(self, sl, t_tgt):
            self.bf_free[sl] = t_tgt

    ws = WStream()

    wait("tensor", "dG", d_xfm_last)
    bank_free = [0] * 8

    def proj_slice(osl, which):
        wsrc = wk_d if which == "k" else wv_d
        base = 0 if which == "k" else 4096
        tiles = {}

        def _load(c):
            tiles[c] = ws.load(wsrc[c * 128:(c + 1) * 128, osl * 512:(osl + 1) * 512])
        for c in range(3):
            _load(c)
        for c in range(CB):
            bf_ap, c_tgt, sl = tiles[c]
            wait("tensor", "sS", c_tgt)
            for t in range(TB):
                if c == 0:
                    wait("tensor", "vS", bank_free[t])
                lhs = S1[:, c * T + t * 128: c * T + (t + 1) * 128]
                top(lambda e, o=PS[t][:], l=lhs, r=bf_ap, st=(c == 0), sp=(c == CB - 1):
                    e.matmul(o, l, r, start=st, stop=sp), inc=(t == TB - 1))
            ws.consumed(sl, P.cur("tS"))
            if c + 3 < CB:
                _load(c + 3)
        mm_done = P.cur("tS")
        evac = []
        for t in range(TB):
            dst = S2[:, base + t * 512: base + (t + 1) * 512]
            wait("vector", "tS", mm_done)
            if which == "k":
                scr = FSCR[:, (t % 2) * 512:(t % 2) * 512 + 512]
                v_min = vop(lambda e, o=scr, p=PS[t][:]: e.tensor_scalar(o, p, 0.0, None, ALU.min))
                wait("scalar", "vS", v_min)
                s_exp = sop(lambda e, o=scr, i_=scr: e.activation(out=o, in_=i_, func=AF.Exp))
                wait("vector", "sS", s_exp)
                ev = vop(lambda e, o=dst, p=PS[t][:], x2=scr:
                         e.scalar_tensor_tensor(o, p, 0.0, x2, ALU.max, ALU.add))
            else:
                ev = vop(lambda e, o=dst, p=PS[t][:]: e.tensor_copy(o, p))
            evac.append(ev)
            bank_free[t] = ev
        return evac

    kv_evac_last = 0
    for osl in range(4):
        ek = proj_slice(osl, "k")
        ev = proj_slice(osl, "v")
        wait("tensor", "vS", max(ek[-1], ev[-1]))
        wait("tensor", "vS", bank_free[0])
        wait("tensor", "vS", bank_free[1])
        for t in range(TB):
            for hh in range(4):
                lhs = S2[:, t * 512 + hh * 128: t * 512 + (hh + 1) * 128]
                rhs = S2[:, 4096 + t * 512 + hh * 128: 4096 + t * 512 + (hh + 1) * 128]
                top(lambda e, o=PS[0][:, hh * 128:(hh + 1) * 128], l=lhs, r=rhs,
                    st=(t == 0 and hh == 0), sp=(t == TB - 1 and hh == 3):
                    e.matmul(o, l, r, start=st, stop=sp, skip_group_check=True), inc=False)
                top(lambda e, o=PS[1][:, hh:hh + 1], l=lhs, r=ONEB[:, 0:1],
                    st=(t == 0 and hh == 0), sp=(t == TB - 1 and hh == 3):
                    e.matmul(o, l, r, start=st, stop=sp, skip_group_check=True),
                    inc=(t == TB - 1 and hh == 3))
        kv_done_t = P.cur("tS")
        wait("vector", "tS", kv_done_t)
        vop(lambda e, o=KVP[:, osl * 512:(osl + 1) * 512], p=PS[0][:]: e.tensor_copy(o, p), inc=False)
        kv_evac_last = vop(lambda e, o=KVP[:, 2048 + osl * 4: 2048 + (osl + 1) * 4],
                           p=PS[1][:, 0:4]: e.tensor_copy(o, p))
        bank_free[0] = kv_evac_last
        bank_free[1] = kv_evac_last
    kv_mm_all = P.cur("tS")

    absum_done = []
    for i in range(32):
        ch, hf = i // 2, i % 2
        sl = i % 2
        f32_ap = WFT[:, sl * 1024:(sl + 1) * 1024]
        if i >= 2:
            wait("scalar", "sS", absum_done[i - 2])
        d_tgt = dma("scalar", "dC", f32_ap,
                    wf_d[ch * 128:(ch + 1) * 128, hf * 1024:(hf + 1) * 1024])
        wait("scalar", "dC", d_tgt)
        a_tgt = sop(lambda e, i_=f32_ap, a=ACC[:, i:i + 1], d=RREG[:, 0:1024]:
                    e.activation(out=d, in_=i_, func=AF.Abs, accum_out=a))
        absum_done.append(a_tgt)
    absum_last = absum_done[-1]

    wait("vector", "sS", absum_last)
    v_asum = vop(lambda e, o=COLS[:, 2:3], i_=ACC[:, 0:32]: e.tensor_reduce(o, i_, axis=AX, op=ALU.add))
    wait("vector", "vS", v_asum)
    wait("sync", "vS", v_asum)
    d_g1 = dma("sync", "dS", ROWS[0:1, 0:128], COLS[:, 2:3])
    wait("vector", "dS", d_g1)
    v_w1 = vop(lambda e, o=ROWS[0:1, 128:129], i_=ROWS[0:1, 0:128]:
               e.tensor_reduce(o, i_, axis=AX, op=ALU.add))
    wait("vector", "vS", v_w1)
    t1_ = vop(lambda e, o=ROWS[0:1, 129:130], i_=ROWS[0:1, 128:129]:
        e.tensor_scalar(o, i_, 1.0 / (C * C), EPS_SCALE, ALU.mult, ALU.max))
    wait("vector", "vS", t1_)
    t2_ = vop(lambda e, o=ROWS[0:1, 130:131], i_=ROWS[0:1, 129:130]: e.reciprocal(o, i_))
    wait("vector", "vS", t2_)
    v_w4 = vop(lambda e, o=ROWS[0:1, 256:384], i_=ONESF[0:1, :]:
               e.tensor_scalar(o, i_, ROWS[0:1, 130:131], None, ALU.mult))
    wait("sync", "vS", v_w4)
    d_invws = dma("sync", "dS", COLS[:, 0:1], ROWS[0:1, 256:384])

    wait("gpsimd", "vS", kv_evac_last)
    d_c1 = dma("gpsimd", "dG", cc1_in.ap(), KVP[:])
    wait("gpsimd", "dG", d_c1)
    cc1_tgt = P.inc("cc", 1)
    P.emit("gpsimd", lambda e: e.collective_compute(
        "AllReduce", ALU.add, replica_groups=[list(range(NCORES))],
        ins=[cc1_in.ap()], outs=[cc1_out.ap()]).then_inc(sems["cc"], 1))
    wait("gpsimd", "cc", cc1_tgt)
    d_c1b = dma("gpsimd", "dG", KVP[:], cc1_out.ap())

    tern_done = []

    class Ternarizer:
        def __init__(self, half, zone, tens_gate):
            self.half, self.zone, self.gate = half, zone, tens_gate
            self.ch = 0

        def emit(self, n):
            for _ in range(n):
                ch = self.ch
                if ch >= CB:
                    return
                self.ch += 1
                sl = ch % 2
                f32_ap = WFT[:, sl * 1024:(sl + 1) * 1024]
                if not tern_done:
                    wait("scalar", "sS", absum_last)
                if len(tern_done) >= 2:
                    wait("scalar", "vS", tern_done[-2])
                d_tgt = dma("scalar", "dC", f32_ap,
                            wf_d[ch * 128:(ch + 1) * 128,
                                 self.half * 1024:(self.half + 1) * 1024])
                wait("vector", "dC", d_tgt)
                if ch == 0:
                    wait("vector", "dS", d_invws)
                    wait("vector", "tS", self.gate)
                dst = self.zone[:, ch * 1024:(ch + 1) * 1024]
                vop(lambda e, o=f32_ap, i_=f32_ap:
                    e.tensor_scalar(o, i_, COLS[:, 0:1], MAGIC, ALU.mult, ALU.add), inc=False)
                vop(lambda e, o=f32_ap, i_=f32_ap:
                    e.tensor_scalar(o, i_, -MAGIC, -1.0, ALU.add, ALU.max), inc=False)
                t3 = vop(lambda e, o=dst, i_=f32_ap: e.tensor_scalar(o, i_, 1.0, None, ALU.min))
                tern_done.append(t3)

    tern1 = Ternarizer(0, S2, kv_mm_all)
    qb_free = list(bank_free)
    for grp in range(4):
        tiles = {}

        def _loadq(c):
            tiles[c] = ws.load(wq_d[c * 128:(c + 1) * 128, grp * 512:(grp + 1) * 512])
        for c in range(3):
            _loadq(c)
        for c in range(CB):
            bf_ap, c_tgt, sl = tiles[c]
            wait("tensor", "sS", c_tgt)
            for j in range(4):
                for th in range(2):
                    bank = 2 * j + th
                    if c == 0:
                        wait("tensor", "vS", qb_free[bank])
                    lhs = bf_ap[:, j * 128:(j + 1) * 128]
                    rhs = S1[:, c * T + th * 512: c * T + (th + 1) * 512]
                    top(lambda e, o=PS[bank][:], l=lhs, r=rhs, st=(c == 0), sp=(c == CB - 1):
                        e.matmul(o, l, r, start=st, stop=sp),
                        inc=(j == 3 and th == 1))
            ws.consumed(sl, P.cur("tS"))
            if c + 3 < CB:
                _loadq(c + 3)
        q_done = P.cur("tS")
        for j in range(4):
            h = grp * 4 + j
            for th in range(2):
                bank = 2 * j + th
                dst = S3[:, h * T + th * 512: h * T + (th + 1) * 512]
                scr = FSCR[:, (bank % 2) * 512:(bank % 2) * 512 + 512]
                wait("vector", "tS", q_done)
                v_min = vop(lambda e, o=scr, p=PS[bank][:]: e.tensor_scalar(o, p, 0.0, None, ALU.min))
                wait("scalar", "vS", v_min)
                s_exp = sop(lambda e, o=scr, i_=scr: e.activation(out=o, in_=i_, func=AF.Exp))
                wait("vector", "sS", s_exp)
                ev = vop(lambda e, o=dst, p=PS[bank][:], x2=scr:
                         e.scalar_tensor_tensor(o, p, 0.0, x2, ALU.max, ALU.add))
                qb_free[bank] = ev
        tern1.emit(4)
    q_mm_all = P.cur("tS")
    q_evac_all = P.cur("vS")

    wait("vector", "dG", d_c1b)
    v_kvbf = vop(lambda e, o=KVSB[:, 0:2064], i_=KVP[:]: e.tensor_copy(o, i_))

    wait("tensor", "vS", v_kvbf)
    wait("tensor", "vS", max(qb_free))
    for h in range(H):
        for t in range(TB):
            lhs = S3[:, h * T + t * 128: h * T + (t + 1) * 128]
            col = t * 16 + h
            top(lambda e, o=PS[2][:, col:col + 1], l=lhs, r=KVSB[:, 2048 + h:2049 + h]:
                e.matmul(o, l, r, start=True, stop=True, skip_group_check=True),
                inc=(h == H - 1 and t == TB - 1))
    den_done = P.cur("tS")
    wait("vector", "tS", den_done)
    t1_ = vop(lambda e, o=ZTOK[:], p=PS[2][:, 0:128]: e.tensor_scalar(o, p, EPS_Z, None, ALU.add))
    wait("vector", "vS", t1_)
    t2_ = vop(lambda e, o=ZTOK[:], i_=ZTOK[:]: e.reciprocal(o, i_))
    wait("vector", "vS", t2_)
    t3_ = vop(lambda e, o=ZB[:, 0:128], i_=ZTOK[:]: e.tensor_copy(o, i_))
    wait("vector", "vS", t3_)
    v_zlo = vop(lambda e, o=ZB[:, 128:256], a=ZTOK[:], b=ZB[:, 0:128]:
                e.tensor_tensor(o, a, b, ALU.subtract))
    wait("tensor", "vS", v_zlo)
    top(lambda e, o=PT[0][:, 0:128], i_=ZB[:, 0:128], iden=IDB[:, 0:128]:
        e.transpose(o, i_, iden), inc=False)
    tr_z = top(lambda e, o=PT[0][:, 128:256], i_=ZB[:, 128:256], iden=IDB[:, 0:128]:
               e.transpose(o, i_, iden))
    wait("scalar", "tS", tr_z)
    sop(lambda e, o=ZB[:, 256:384], p=PT[0][:, 0:128]: e.copy(out=o, in_=p), inc=False)
    s_zrows = sop(lambda e, o=ZB[:, 384:512], p=PT[0][:, 128:256]: e.copy(out=o, in_=p))

    zhi_rows = ZB[:, 256:384]
    zlo_rows = ZB[:, 384:512]
    wait("tensor", "sS", s_zrows)
    wait("sync", "sS", s_zrows)
    pair_gate = [0, 0]
    zr_gate = 0
    for h in range(H):
        pair = h % 2
        zb0, zb1 = (4, 5) if pair == 0 else (2, 3)
        ob0 = 6 if pair == 0 else 0
        wait("sync", "tS", zr_gate)
        dma("sync", "dS", ZR[0:1, 0:1024], zhi_rows[h:128:16, 0:128])
        d_zr = dma("sync", "dS", ZR[0:1, 1024:2048], zlo_rows[h:128:16, 0:128])
        wait("tensor", "vS", pair_gate[pair])
        wait("tensor", "dS", d_zr)
        for half, bank in ((0, zb0), (1, zb1)):
            top(lambda e, o=PS[bank][:], l=ONEB[0:1, 0:128],
                r=ZR[0:1, half * 512: (half + 1) * 512]:
                e.matmul(o, l, r, start=True, stop=False, skip_group_check=True), inc=False)
            top(lambda e, o=PS[bank][:], l=ONEB[0:1, 0:128],
                r=ZR[0:1, 1024 + half * 512: 1024 + (half + 1) * 512]:
                e.matmul(o, l, r, start=False, stop=True, skip_group_check=True),
                inc=(half == 1))
        zbc_done = P.cur("tS")
        zr_gate = zbc_done
        for th in range(2):
            top(lambda e, o=PS[ob0 + th][:], l=KVSB[:, h * 128:(h + 1) * 128],
                r=S3[:, h * T + th * 512: h * T + (th + 1) * 512]:
                e.matmul(o, l, r, start=True, stop=True), inc=(th == 1))
        out_done = P.cur("tS")
        wait("scalar", "tS", zbc_done)
        sop(lambda e, o=ZBC[:, 0:512], p=PS[zb0][:]: e.copy(out=o, in_=p), inc=False)
        s_zb = sop(lambda e, o=ZBC[:, 512:1024], p=PS[zb1][:]: e.copy(out=o, in_=p))
        wait("vector", "sS", s_zb)
        wait("vector", "tS", out_done)
        ev = 0
        for th in range(2):
            ev = vop(lambda e, o=S3[:, h * T + th * 512: h * T + (th + 1) * 512],
                     p=PS[ob0 + th][:], zz=ZBC[:, th * 512:(th + 1) * 512]:
                     e.tensor_tensor(o, p, zz, ALU.mult))
        pair_gate[pair] = ev
    out_phase_v = P.cur("vS")

    wait("sync", "sS", absum_last)
    wait("sync", "vS", v_kvbf)
    d_xtok = [dma("sync", "dS", RREG[:, t * C:(t + 1) * C], x_tok_d[t * 128:(t + 1) * 128, :])
              for t in range(TB)]

    tern2 = Ternarizer(1, S1, q_mm_all)

    wait("tensor", "vS", out_phase_v)
    wait("tensor", "dS", d_xtok[-1])
    bank_free2 = [0] * 8
    for osl in range(4):
        tiles = {}

        def _loadw(c):
            tiles[c] = ws.load(wo_d[c * 128:(c + 1) * 128, osl * 512:(osl + 1) * 512])
        for c in range(3):
            _loadw(c)
        for c in range(CB):
            bf_ap, c_tgt, sl = tiles[c]
            wait("tensor", "sS", c_tgt)
            for t in range(TB):
                if c == 0:
                    wait("tensor", "vS", bank_free2[t])
                lhs = S3[:, c * T + t * 128: c * T + (t + 1) * 128]
                top(lambda e, o=PS[t][:], l=lhs, r=bf_ap, st=(c == 0), sp=(c == CB - 1):
                    e.matmul(o, l, r, start=st, stop=sp), inc=(t == TB - 1))
            ws.consumed(sl, P.cur("tS"))
            if c + 3 < CB:
                _loadw(c + 3)
        wo_done = P.cur("tS")
        for t in range(TB):
            wait("vector", "tS", wo_done)
            dst = RREG[:, t * C + osl * 512: t * C + (osl + 1) * 512]
            ev = vop(lambda e, o=dst, p=PS[t][:]:
                     e.scalar_tensor_tensor(o, p, 1.0, o, ALU.mult, ALU.add))
            bank_free2[t] = ev
        tern2.emit(4)
    wo_mm_all = P.cur("tS")
    tern_all = tern_done[-1]

    for t in range(TB):
        for g in range(4):
            vop(lambda e, o=BNS[:, g * 6:(g + 1) * 6],
                i_=RREG[:, t * C + g * 512: t * C + (g + 1) * 512]:
                e.bn_stats(o, i_), inc=(g == 3))
        bn_tgt = P.cur("vS")
        wait("vector", "vS", bn_tgt)
        for fn_ in (
            lambda e, t=t: e.bn_aggr(STA[:, t * 8:t * 8 + 2], BNS[:, 0:24]),
            lambda e, t=t: e.tensor_scalar(STA[:, t * 8 + 2:t * 8 + 3],
                                           STA[:, t * 8 + 1:t * 8 + 2], EPS_LN, None, ALU.add),
            lambda e, t=t: e.reciprocal(STA[:, t * 8 + 3:t * 8 + 4], STA[:, t * 8 + 2:t * 8 + 3]),
            lambda e, t=t: e.tensor_tensor(STA[:, t * 8 + 4:t * 8 + 5], STA[:, t * 8 + 1:t * 8 + 2],
                                           STA[:, t * 8 + 3:t * 8 + 4], ALU.mult),
            lambda e, t=t: e.tensor_scalar(STA[:, t * 8 + 4:t * 8 + 5], STA[:, t * 8 + 4:t * 8 + 5],
                                           EPS_RMS, None, ALU.add),
        ):
            v_m2 = vop(fn_)
            wait("vector", "vS", v_m2)
        wait("scalar", "vS", v_m2)
        sop(lambda e, o=STA[:, t * 8 + 2:t * 8 + 3], i_=STA[:, t * 8 + 2:t * 8 + 3]:
            e.activation(out=o, in_=i_, func=AF.Sqrt), inc=False)
        s_sq = sop(lambda e, o=STA[:, t * 8 + 4:t * 8 + 5], i_=STA[:, t * 8 + 4:t * 8 + 5]:
                   e.activation(out=o, in_=i_, func=AF.Sqrt))
        wait("vector", "sS", s_sq)
        t_ = vop(lambda e, o=STA[:, t * 8 + 2:t * 8 + 3], i_=STA[:, t * 8 + 2:t * 8 + 3]:
                 e.reciprocal(o, i_))
        wait("vector", "vS", t_)
        t_ = vop(lambda e, o=STA[:, t * 8 + 4:t * 8 + 5], i_=STA[:, t * 8 + 4:t * 8 + 5]:
                 e.reciprocal(o, i_))
        wait("vector", "vS", t_)
        vop(lambda e, o=RREG[:, t * C:(t + 1) * C], i_=RREG[:, t * C:(t + 1) * C],
            m=STA[:, t * 8:t * 8 + 1], s2=STA[:, t * 8 + 2:t * 8 + 3]:
            e.tensor_scalar(o, i_, m, s2, ALU.subtract, ALU.mult), inc=False)
        if t == 0:
            wait("vector", "dS", d_gam)
        v_xg = vop(lambda e, o=WFT[:, 0:2048], a=RREG[:, t * C:(t + 1) * C], b=GAM[:]:
                   e.tensor_tensor(o, a, b, ALU.mult))
        wait("vector", "vS", v_xg)
        t_ = vop(lambda e, o=STA[:, t * 8 + 5:t * 8 + 6], i_=WFT[:, 0:2048]:
                 e.tensor_reduce(o, i_, axis=AX, op=ALU.max, apply_absolute_value=True))
        wait("vector", "vS", t_)
        vop(lambda e, o=STA[:, t * 8 + 5:t * 8 + 6], a=STA[:, t * 8 + 5:t * 8 + 6],
            b=STA[:, t * 8 + 4:t * 8 + 5]: e.tensor_tensor(o, a, b, ALU.mult),
            inc=(t == TB - 1))
    v_cmb = vop(lambda e, o=COLS[:, 3:4], i_=STA[:, 5:6]: e.tensor_copy(o, i_))
    for t in range(1, TB):
        wait("vector", "vS", v_cmb)
        v_cmb = vop(lambda e, o=COLS[:, 3:4], a=COLS[:, 3:4], b=STA[:, t * 8 + 5:t * 8 + 6]:
                    e.tensor_tensor(o, a, b, ALU.max))
    wait("sync", "vS", v_cmb)
    d_g2 = dma("sync", "dS", ROWS[0:1, 0:128], COLS[:, 3:4])

    wait("gpsimd", "dS", d_g2)
    d_c2 = dma("gpsimd", "dG", cc2_in.ap(), ROWS[0:1, 0:128])
    wait("gpsimd", "dG", d_c2)
    cc2_tgt = P.inc("cc", 1)
    P.emit("gpsimd", lambda e: e.collective_compute(
        "AllReduce", ALU.max, replica_groups=[list(range(NCORES))],
        ins=[cc2_in.ap()], outs=[cc2_out.ap()]).then_inc(sems["cc"], 1))
    wait("gpsimd", "cc", cc2_tgt)
    d_c2b = dma("gpsimd", "dG", ROWS[0:1, 0:128], cc2_out.ap())
    wait("vector", "dG", d_c2b)
    v_s1 = vop(lambda e, o=ROWS[0:1, 131:132], i_=ROWS[0:1, 0:128]:
               e.tensor_reduce(o, i_, axis=AX, op=ALU.max))
    wait("vector", "vS", v_s1)
    for fn_ in (
        lambda e: e.tensor_scalar(ROWS[0:1, 131:132], ROWS[0:1, 131:132], EPS_SCALE, None, ALU.max),
        lambda e: e.reciprocal(ROWS[0:1, 132:133], ROWS[0:1, 131:132]),
        lambda e: e.tensor_scalar(ROWS[0:1, 133:134], ROWS[0:1, 132:133], 127.0, None, ALU.mult),
        lambda e: e.tensor_tensor(ROWS[0:1, 134:135], ROWS[0:1, 131:132], ROWS[0:1, 129:130], ALU.mult),
        lambda e: e.tensor_scalar(ROWS[0:1, 134:135], ROWS[0:1, 134:135], 1.0 / 127.0, None, ALU.mult),
        lambda e: e.tensor_scalar(ROWS[0:1, 256:384], ONESF[0:1, :], ROWS[0:1, 133:134], None, ALU.mult),
        lambda e: e.tensor_scalar(ROWS[0:1, 384:512], ONESF[0:1, :], ROWS[0:1, 134:135], None, ALU.mult),
    ):
        t_ = vop(fn_)
        wait("vector", "vS", t_)
    v_s7 = t_
    wait("sync", "vS", v_s7)
    dma("sync", "dS", COLS[:, 4:5], ROWS[0:1, 256:384])
    d_sc3 = dma("sync", "dS", COLS[:, 5:6], ROWS[0:1, 384:512])
    wait("vector", "dS", d_sc3)
    b_done = 0
    for t in range(TB):
        b_done = vop(lambda e, o=STA[:, t * 8 + 6:t * 8 + 7], a=STA[:, t * 8 + 4:t * 8 + 5],
                     b=COLS[:, 4:5]: e.tensor_tensor(o, a, b, ALU.mult), inc=(t == TB - 1))

    wait("vector", "tS", wo_mm_all)
    wait("tensor", "vS", max(b_done, tern_all))
    ntok_free = [0, 0]
    nfm_free = [0, 0, 0]
    ffn_bank_free = [0] * 8
    tr_evac = {}
    last_store = 0
    for t in range(TB):
        tok_sl = t % 2
        fm_sl = t % 3
        tok_ap = S3[:, tok_sl * 2048:(tok_sl + 1) * 2048]
        fm_base = 4096 + fm_sl * 2048
        wait("vector", "tS", ntok_free[tok_sl])
        vop(lambda e, o=WFT[:, 0:2048], i_=RREG[:, t * C:(t + 1) * C],
            b=STA[:, t * 8 + 6:t * 8 + 7], g=GAM[:]:
            e.scalar_tensor_tensor(o, i_, b, g, ALU.mult, ALU.mult), inc=False)
        vop(lambda e, o=WFT[:, 0:2048], i_=WFT[:, 0:2048]:
            e.tensor_scalar(o, i_, -128.0, 127.0, ALU.max, ALU.min), inc=False)
        v_q = vop(lambda e, o=tok_ap, i_=WFT[:, 0:2048]:
                  e.tensor_scalar(o, i_, MAGIC, -MAGIC, ALU.add, ALU.add))
        wait("tensor", "vS", v_q)
        for c in range(CB):
            bank = c % 2
            off = ((c // 2) % 4) * 128
            if (t, c - 2) in tr_evac:
                wait("tensor", "sS", tr_evac[(t, c - 2)])
            elif c < 2 and t > 0 and (t - 1, 14 + c) in tr_evac:
                wait("tensor", "sS", tr_evac[(t - 1, 14 + c)])
            tr = top(lambda e, o=PT[bank][:, off:off + 128],
                     i_=tok_ap[:, c * 128:(c + 1) * 128], iden=IDB[:, 0:128]:
                     e.transpose(o, i_, iden))
            wait("scalar", "tS", tr)
            tr_evac[(t, c)] = sop(lambda e, o=S3[:, fm_base + c * 128: fm_base + (c + 1) * 128],
                                  p=PT[bank][:, off:off + 128]: e.copy(out=o, in_=p))
        ntok_free[tok_sl] = P.cur("tS")
        trans_s = tr_evac[(t, CB - 1)]
        wait("tensor", "sS", trans_s)
        pb = 0
        for osl in range(4):
            wait("tensor", "vS", ffn_bank_free[pb + osl])
        for c in range(CB):
            lhs = S3[:, fm_base + c * 128: fm_base + (c + 1) * 128]
            for osl in range(4):
                zone = S2 if osl < 2 else S1
                rhs = zone[:, c * 1024 + (osl % 2) * 512: c * 1024 + (osl % 2) * 512 + 512]
                top(lambda e, o=PS[pb + osl][:], l=lhs, r=rhs, st=(c == 0), sp=(c == CB - 1):
                    e.matmul(o, l, r, start=st, stop=sp),
                    inc=(c == CB - 1 and osl == 3))
        ffn_done = P.cur("tS")
        nfm_free[fm_sl] = ffn_done
        wait("vector", "tS", ffn_done)
        for osl in range(4):
            ev = vop(lambda e, o=RREG[:, t * C + osl * 512: t * C + (osl + 1) * 512],
                     p=PS[pb + osl][:], sc=COLS[:, 5:6]:
                     e.scalar_tensor_tensor(o, p, sc, o, ALU.mult, ALU.add))
            ffn_bank_free[pb + osl] = ev
        for g in range(4):
            vop(lambda e, o=BNS[:, g * 6:(g + 1) * 6],
                i_=RREG[:, t * C + g * 512: t * C + (g + 1) * 512]:
                e.bn_stats(o, i_), inc=(g == 3))
        bn2 = P.cur("vS")
        wait("vector", "vS", bn2)
        t_ = vop(lambda e, o=STA[:, t * 8 + 6:t * 8 + 8], i_=BNS[:, 0:24]: e.bn_aggr(o, i_))
        wait("vector", "vS", t_)
        v_u2 = vop(lambda e, o=STA[:, t * 8 + 7:t * 8 + 8], i_=STA[:, t * 8 + 7:t * 8 + 8]:
                   e.tensor_scalar(o, i_, EPS_LN, None, ALU.add))
        wait("scalar", "vS", v_u2)
        s_sq2 = sop(lambda e, o=STA[:, t * 8 + 7:t * 8 + 8], i_=STA[:, t * 8 + 7:t * 8 + 8]:
                    e.activation(out=o, in_=i_, func=AF.Sqrt))
        wait("vector", "sS", s_sq2)
        t_ = vop(lambda e, o=STA[:, t * 8 + 7:t * 8 + 8], i_=STA[:, t * 8 + 7:t * 8 + 8]:
                 e.reciprocal(o, i_))
        wait("vector", "vS", t_)
        v_out = vop(lambda e, o=RREG[:, t * C:(t + 1) * C], i_=RREG[:, t * C:(t + 1) * C],
                    m=STA[:, t * 8 + 6:t * 8 + 7], s2=STA[:, t * 8 + 7:t * 8 + 8]:
                    e.tensor_scalar(o, i_, m, s2, ALU.subtract, ALU.mult))
        wait("sync", "vS", v_out)
        last_store = dma("sync", "dS", y_d[t * 128:(t + 1) * 128, :], RREG[:, t * C:(t + 1) * C])

    wait("sync", "dS", last_store)
    wait("gpsimd", "dS", last_store)

    with es:
        block = es.enter_context(nc.Block())

        @block.sync
        def _(e):
            for f in P.ins["sync"]:
                f(e)

        @block.gpsimd
        def _(e):
            for f in P.ins["gpsimd"]:
                f(e)

        @block.tensor
        def _(e):
            for f in P.ins["tensor"]:
                f(e)

        @block.vector
        def _(e):
            for f in P.ins["vector"]:
                f(e)

        @block.scalar
        def _(e):
            for f in P.ins["scalar"]:
                f(e)

    return nc


def kernel(L0_attn_x, L0_attn_cos, L0_attn_sin, L0_attn_Wq, L0_attn_Wk, L0_attn_Wv,
           L0_attn_Wo, L0_ffn_weight, L0_ffn_norm_weight):
    x = np.ascontiguousarray(np.asarray(L0_attn_x, dtype=np.float32)[0])
    xT = np.ascontiguousarray(x.T)
    wq_t = np.ascontiguousarray(np.asarray(L0_attn_Wq, dtype=np.float32).T)
    wk_t = np.ascontiguousarray(np.asarray(L0_attn_Wk, dtype=np.float32).T)
    wv_t = np.ascontiguousarray(np.asarray(L0_attn_Wv, dtype=np.float32).T)
    wo_t = np.ascontiguousarray(np.asarray(L0_attn_Wo, dtype=np.float32).T)
    wf_t = np.ascontiguousarray(np.asarray(L0_ffn_weight, dtype=np.float32).T)
    gam = np.ascontiguousarray(np.broadcast_to(
        np.asarray(L0_ffn_norm_weight, dtype=np.float32)[None, :], (128, C)))
    iden = np.eye(128, dtype=np.float32)
    ones = np.ones((128, 128), dtype=np.float32)

    nc = build_nc()
    in_maps = []
    for cid in range(NCORES):
        in_maps.append({
            "x_fm": np.ascontiguousarray(xT[:, cid * T:(cid + 1) * T]),
            "x_tok": np.ascontiguousarray(x[cid * T:(cid + 1) * T, :]),
            "wq_t": wq_t, "wk_t": wk_t, "wv_t": wv_t, "wo_t": wo_t, "wf_t": wf_t,
            "gamma_bc": gam, "id_bf": iden, "ones_f": ones,
        })
    res = run_bass_kernel_spmd(nc, in_maps, core_ids=list(range(NCORES)))
    out = np.concatenate([res.results[i]["y"] for i in range(NCORES)], axis=0)
    return out.reshape(1, 8192, C)


# version 07
# speedup vs baseline: 1.0577x; 1.0577x over previous
import contextlib
import numpy as np
import concourse.bass as bass
import concourse.mybir as mybir
from concourse.bass_utils import run_bass_kernel_spmd

F32 = mybir.dt.float32
BF16 = mybir.dt.bfloat16
AF = mybir.ActivationFunctionType
ALU = mybir.AluOpType
AX = mybir.AxisListType.X

NCORES = 8
T = 1024
TB = 8
C = 2048
CB = 16
H = 16
MAGIC = 12582912.0
EPS_LN = 1e-5
EPS_RMS = 1e-6
EPS_SCALE = 1e-5
EPS_Z = 1e-6


class Prog:
    ENGINES = ("sync", "gpsimd", "tensor", "vector", "scalar")

    def __init__(self):
        self.ins = {e: [] for e in self.ENGINES}
        self.sem_ct = {}

    def emit(self, eng, fn):
        self.ins[eng].append(fn)

    def inc(self, sem, amount):
        self.sem_ct[sem] = self.sem_ct.get(sem, 0) + amount
        return self.sem_ct[sem]

    def cur(self, sem):
        return self.sem_ct.get(sem, 0)


def build_nc():
    nc = bass.Bass("TRN2", target_bir_lowering=False, debug=False)

    x_fm_d = nc.dram_tensor("x_fm", [C, T], F32, kind="ExternalInput").ap()
    x_tok_d = nc.dram_tensor("x_tok", [T, C], F32, kind="ExternalInput").ap()
    wq_d = nc.dram_tensor("wq_t", [C, C], F32, kind="ExternalInput").ap()
    wk_d = nc.dram_tensor("wk_t", [C, C], F32, kind="ExternalInput").ap()
    wv_d = nc.dram_tensor("wv_t", [C, C], F32, kind="ExternalInput").ap()
    wo_d = nc.dram_tensor("wo_t", [C, C], F32, kind="ExternalInput").ap()
    wf_d = nc.dram_tensor("wf_t", [C, C], F32, kind="ExternalInput").ap()
    gam_d = nc.dram_tensor("gamma_bc", [128, C], F32, kind="ExternalInput").ap()
    id_d = nc.dram_tensor("id_bf", [128, 128], F32, kind="ExternalInput").ap()
    ones_d = nc.dram_tensor("ones_f", [128, 128], F32, kind="ExternalInput").ap()
    y_d = nc.dram_tensor("y", [T, C], F32, kind="ExternalOutput").ap()

    cc1_in = nc.dram_tensor("cc1_in", [128, 2064], F32)
    cc1_out = nc.dram_tensor("cc1_out", [128, 2064], F32)
    cc2_in = nc.dram_tensor("cc2_in", [1, 128], F32)
    cc2_out = nc.dram_tensor("cc2_out", [1, 128], F32)

    P = Prog()
    es = contextlib.ExitStack()

    def sb(name, shape, dtype=F32):
        return es.enter_context(nc.sbuf_tensor(name, shape, dtype))

    RREG = sb("RREG", [128, TB * C])
    GAM = sb("GAM", [128, C])
    WFT = sb("WFT", [128, C])
    FSCR = sb("FSCR", [128, 1024])
    WST = sb("WST", [128, 1536])
    S1f = sb("S1f", [128, 8192])
    S2f = sb("S2f", [128, 8192])
    S3f = sb("S3f", [128, 8192])
    S1 = S1f.bitcast(BF16)
    S2 = S2f.bitcast(BF16)
    S3 = S3f.bitcast(BF16)
    WSTB = sb("WSTB", [128, 768]).bitcast(BF16)
    KVSB = sb("KVSB", [128, 1032]).bitcast(BF16)
    ZB = sb("ZB", [128, 256]).bitcast(BF16)
    IDB = sb("IDB", [128, 64]).bitcast(BF16)
    ONEB = sb("ONEB", [128, 64]).bitcast(BF16)
    ZBC = sb("ZBC", [128, T])
    ZTOK = sb("ZTOK", [128, 128])
    BNS = sb("BNS", [128, 24])
    STA = sb("STA", [128, 64])
    ACC = sb("ACC", [128, 32])
    COLS = sb("COLS", [128, 8])
    ROWS = sb("ROWS", [1, 512])
    ZR = sb("ZR", [1, 1024]).bitcast(BF16)
    ONESF = sb("ONESF", [1, 128])

    KVP = RREG[:, 1024:3088]

    PS = [es.enter_context(nc.psum_tensor(f"ps{i}", [128, 512], F32)) for i in range(6)]
    PT = [es.enter_context(nc.psum_tensor(f"pt{i}", [128, 1024], BF16)) for i in range(2)]
    PS.append(PT[0].bitcast(F32))
    PS.append(PT[1].bitcast(F32))

    sems = {}
    for s in ("dS", "dG", "dC", "tS", "vS", "sS", "cc",
              "dW0", "dW1", "dW2", "dF0", "dF1", "dX", "dZ"):
        sems[s] = es.enter_context(nc.semaphore(s))

    def dma(eng, qsem, out, in_):
        tgt = P.inc(qsem, 16)
        P.emit(eng, lambda e, o=out, i_=in_, s=sems[qsem]: e.dma_start(out=o, in_=i_).then_inc(s, 16))
        return tgt

    def wait(eng, sem, tgt):
        if tgt and tgt > 0:
            P.emit(eng, lambda e, s=sems[sem], t=tgt: e.wait_ge(s, t))

    def vop(fn, inc=True):
        if inc:
            tgt = P.inc("vS", 1)
            P.emit("vector", lambda e, f=fn, s=sems["vS"]: f(e).then_inc(s, 1))
            return tgt
        P.emit("vector", lambda e, f=fn: f(e))
        return P.cur("vS")

    def sop(fn, inc=True):
        if inc:
            tgt = P.inc("sS", 1)
            P.emit("scalar", lambda e, f=fn, s=sems["sS"]: f(e).then_inc(s, 1))
            return tgt
        P.emit("scalar", lambda e, f=fn: f(e))
        return P.cur("sS")

    def top(fn, inc=True):
        if inc:
            tgt = P.inc("tS", 1)
            P.emit("tensor", lambda e, f=fn, s=sems["tS"]: f(e).then_inc(s, 1))
            return tgt
        P.emit("tensor", lambda e, f=fn: f(e))
        return P.cur("tS")

    d_id = dma("gpsimd", "dG", IDB[:, 0:128], id_d)
    d_ones = dma("gpsimd", "dG", ONEB[:, 0:128], ones_d)
    d_onesf = dma("sync", "dS", ONESF[0:1, :], ones_d[0:1, :])
    d_gam = dma("sync", "dS", GAM[:], gam_d)
    d_xfm_last = 0
    for c in range(CB):
        d_xfm_last = dma("gpsimd", "dG", S1[:, c * T:(c + 1) * T], x_fm_d[c * 128:(c + 1) * 128, :])

    class WStream:
        def __init__(self):
            self.i = 0
            self.f32_free = [0, 0, 0]
            self.bf_free = [0, 0, 0]

        def load(self, src_ap):
            sl = self.i % 3
            self.i += 1
            f32_ap = WST[:, sl * 512:(sl + 1) * 512]
            bf_ap = WSTB[:, sl * 512:(sl + 1) * 512]
            wait("sync", "sS", self.f32_free[sl])
            d_tgt = dma("sync", f"dW{sl}", f32_ap, src_ap)
            wait("scalar", f"dW{sl}", d_tgt)
            wait("scalar", "tS", self.bf_free[sl])
            c_tgt = sop(lambda e, o=bf_ap, i_=f32_ap: e.copy(out=o, in_=i_))
            self.f32_free[sl] = c_tgt
            return bf_ap, c_tgt, sl

        def consumed(self, sl, t_tgt):
            self.bf_free[sl] = t_tgt

    ws = WStream()

    wait("tensor", "dG", d_xfm_last)
    bank_free = [0] * 8

    def proj_slice(osl, which):
        wsrc = wk_d if which == "k" else wv_d
        base = 0 if which == "k" else 4096
        tiles = {}

        def _load(c):
            tiles[c] = ws.load(wsrc[c * 128:(c + 1) * 128, osl * 512:(osl + 1) * 512])
        for c in range(3):
            _load(c)
        for c in range(CB):
            bf_ap, c_tgt, sl = tiles[c]
            wait("tensor", "sS", c_tgt)
            for t in range(TB):
                if c == 0:
                    wait("tensor", "vS", bank_free[t])
                lhs = S1[:, c * T + t * 128: c * T + (t + 1) * 128]
                top(lambda e, o=PS[t][:], l=lhs, r=bf_ap, st=(c == 0), sp=(c == CB - 1):
                    e.matmul(o, l, r, start=st, stop=sp), inc=(t == TB - 1))
            ws.consumed(sl, P.cur("tS"))
            if c + 3 < CB:
                _load(c + 3)
        mm_done = P.cur("tS")
        evac = [0] * TB
        if which == "k":
            exps = [0] * TB

            def _stt(t):
                wait("vector", "sS", exps[t])
                dstt = S2[:, base + t * 512: base + (t + 1) * 512]
                scr2 = FSCR[:, (t % 2) * 512:(t % 2) * 512 + 512]
                evac[t] = vop(lambda e, o=dstt, p=PS[t][:], x2=scr2:
                              e.scalar_tensor_tensor(o, p, 0.0, x2, ALU.max, ALU.add))
                bank_free[t] = evac[t]
            wait("vector", "tS", mm_done)
            for t in range(TB):
                scr = FSCR[:, (t % 2) * 512:(t % 2) * 512 + 512]
                v_min = vop(lambda e, o=scr, p=PS[t][:]: e.tensor_scalar(o, p, 0.0, None, ALU.min))
                if t >= 1:
                    _stt(t - 1)
                wait("scalar", "vS", v_min)
                exps[t] = sop(lambda e, o=scr, i_=scr: e.activation(out=o, in_=i_, func=AF.Exp))
            _stt(TB - 1)
        else:
            wait("vector", "tS", mm_done)
            for t in range(TB):
                dst = S2[:, base + t * 512: base + (t + 1) * 512]
                ev = vop(lambda e, o=dst, p=PS[t][:]: e.tensor_copy(o, p))
                evac[t] = ev
                bank_free[t] = ev
        return evac

    class Absum:
        def __init__(self):
            self.di = 0
            self.ci = 0
            self.dtgt = {}
            self.rtgt = {}

        def _dma_one(self):
            i = self.di
            self.di += 1
            ch, hf = i // 2, i % 2
            if i >= 2:
                wait("scalar", "vS", self.rtgt[i - 2])
            self.dtgt[i] = dma("scalar", f"dF{i % 2}", WFT[:, (i % 2) * 1024:(i % 2) * 1024 + 1024],
                               wf_d[ch * 128:(ch + 1) * 128, hf * 1024:(hf + 1) * 1024])

        def emit(self, n):
            for _ in range(n):
                if self.ci >= 32:
                    return
                while self.di < min(self.ci + 2, 32):
                    self._dma_one()
                i = self.ci
                self.ci += 1
                wait("vector", f"dF{i % 2}", self.dtgt[i])
                self.rtgt[i] = vop(lambda e, o=ACC[:, i:i + 1],
                                   i_=WFT[:, (i % 2) * 1024:(i % 2) * 1024 + 1024]:
                                   e.tensor_reduce(o, i_, axis=AX, op=ALU.add,
                                                   apply_absolute_value=True))
                while self.di < min(self.ci + 2, 32):
                    self._dma_one()

    absum = Absum()

    kv_evac_last = 0
    for osl in range(4):
        ek = proj_slice(osl, "k")
        absum.emit(4)
        ev = proj_slice(osl, "v")
        absum.emit(4)
        wait("tensor", "vS", bank_free[0])
        wait("tensor", "vS", bank_free[1])
        wait("tensor", "vS", max(ek[-1], ev[-1]))
        for t in range(TB):
            for hh in range(4):
                lhs = S2[:, t * 512 + hh * 128: t * 512 + (hh + 1) * 128]
                rhs = S2[:, 4096 + t * 512 + hh * 128: 4096 + t * 512 + (hh + 1) * 128]
                top(lambda e, o=PS[0][:, hh * 128:(hh + 1) * 128], l=lhs, r=rhs,
                    st=(t == 0 and hh == 0), sp=(t == TB - 1 and hh == 3):
                    e.matmul(o, l, r, start=st, stop=sp, skip_group_check=True), inc=False)
                top(lambda e, o=PS[1][:, hh:hh + 1], l=lhs, r=ONEB[:, 0:1],
                    st=(t == 0 and hh == 0), sp=(t == TB - 1 and hh == 3):
                    e.matmul(o, l, r, start=st, stop=sp, skip_group_check=True),
                    inc=(t == TB - 1 and hh == 3))
        kv_done_t = P.cur("tS")
        wait("vector", "tS", kv_done_t)
        vop(lambda e, o=KVP[:, osl * 512:(osl + 1) * 512], p=PS[0][:]: e.tensor_copy(o, p), inc=False)
        kv_evac_last = vop(lambda e, o=KVP[:, 2048 + osl * 4: 2048 + (osl + 1) * 4],
                           p=PS[1][:, 0:4]: e.tensor_copy(o, p))
        bank_free[0] = kv_evac_last
        bank_free[1] = kv_evac_last
    kv_mm_all = P.cur("tS")
    absum.emit(32)
    absum_last = absum.rtgt[31]

    wait("vector", "vS", absum_last)
    v_asum = vop(lambda e, o=COLS[:, 2:3], i_=ACC[:, 0:32]: e.tensor_reduce(o, i_, axis=AX, op=ALU.add))
    wait("vector", "vS", v_asum)
    wait("sync", "vS", v_asum)
    d_g1 = dma("sync", "dZ", ROWS[0:1, 0:128], COLS[:, 2:3])
    wait("vector", "dZ", d_g1)
    v_w1 = vop(lambda e, o=ROWS[0:1, 128:129], i_=ROWS[0:1, 0:128]:
               e.tensor_reduce(o, i_, axis=AX, op=ALU.add))
    wait("vector", "vS", v_w1)
    t1_ = vop(lambda e, o=ROWS[0:1, 129:130], i_=ROWS[0:1, 128:129]:
        e.tensor_scalar(o, i_, 1.0 / (C * C), EPS_SCALE, ALU.mult, ALU.max))
    wait("vector", "vS", t1_)
    t2_ = vop(lambda e, o=ROWS[0:1, 130:131], i_=ROWS[0:1, 129:130]: e.reciprocal(o, i_))
    wait("vector", "vS", t2_)
    wait("vector", "dS", d_onesf)
    v_w4 = vop(lambda e, o=ROWS[0:1, 256:384], i_=ONESF[0:1, :]:
               e.tensor_scalar(o, i_, ROWS[0:1, 130:131], None, ALU.mult))
    wait("sync", "vS", v_w4)
    d_invws = dma("sync", "dZ", COLS[:, 0:1], ROWS[0:1, 256:384])

    wait("gpsimd", "vS", kv_evac_last)
    d_c1 = dma("gpsimd", "dG", cc1_in.ap(), KVP[:])
    wait("gpsimd", "dG", d_c1)
    cc1_tgt = P.inc("cc", 1)
    P.emit("gpsimd", lambda e: e.collective_compute(
        "AllReduce", ALU.add, replica_groups=[list(range(NCORES))],
        ins=[cc1_in.ap()], outs=[cc1_out.ap()]).then_inc(sems["cc"], 1))
    wait("gpsimd", "cc", cc1_tgt)
    d_c1b = dma("gpsimd", "dG", KVP[:], cc1_out.ap())

    tern_done = []

    class Ternarizer:
        def __init__(self, half, zone, tens_gate):
            self.half, self.zone, self.gate = half, zone, tens_gate
            self.di = 0
            self.ci = 0
            self.dtgt = {}
            self.ctgt = {}

        def _dma_one(self):
            ch = self.di
            self.di += 1
            if ch < 2 and self.half == 0:
                wait("scalar", "vS", absum_last)
            if ch >= 2:
                wait("scalar", "vS", self.ctgt[ch - 2])
            elif self.half == 1 and ch < 2:
                wait("scalar", "vS", tern_done[CB - 2 + ch])
            self.dtgt[ch] = dma("scalar", f"dF{ch % 2}", WFT[:, (ch % 2) * 1024:(ch % 2) * 1024 + 1024],
                                wf_d[ch * 128:(ch + 1) * 128,
                                     self.half * 1024:(self.half + 1) * 1024])

        def emit(self, n):
            for _ in range(n):
                if self.ci >= CB:
                    return
                while self.di < min(self.ci + 2, CB):
                    self._dma_one()
                ch = self.ci
                self.ci += 1
                f32_ap = WFT[:, (ch % 2) * 1024:(ch % 2) * 1024 + 1024]
                wait("vector", f"dF{ch % 2}", self.dtgt[ch])
                if ch == 0:
                    wait("vector", "dZ", d_invws)
                    wait("vector", "tS", self.gate)
                dst = self.zone[:, ch * 1024:(ch + 1) * 1024]
                vop(lambda e, o=f32_ap, i_=f32_ap:
                    e.tensor_scalar(o, i_, COLS[:, 0:1], MAGIC, ALU.mult, ALU.add), inc=False)
                vop(lambda e, o=f32_ap, i_=f32_ap:
                    e.tensor_scalar(o, i_, -MAGIC, -1.0, ALU.add, ALU.max), inc=False)
                t3 = vop(lambda e, o=dst, i_=f32_ap: e.tensor_scalar(o, i_, 1.0, None, ALU.min))
                self.ctgt[ch] = t3
                tern_done.append(t3)
                while self.di < min(self.ci + 2, CB):
                    self._dma_one()

    tern1 = Ternarizer(0, S2, kv_mm_all)
    qb_free = list(bank_free)
    for grp in range(4):
        tiles = {}

        def _loadq(c):
            tiles[c] = ws.load(wq_d[c * 128:(c + 1) * 128, grp * 512:(grp + 1) * 512])
        for c in range(3):
            _loadq(c)
        for c in range(CB):
            bf_ap, c_tgt, sl = tiles[c]
            wait("tensor", "sS", c_tgt)
            for j in range(4):
                for th in range(2):
                    bank = 2 * j + th
                    if c == 0:
                        wait("tensor", "vS", qb_free[bank])
                    lhs = bf_ap[:, j * 128:(j + 1) * 128]
                    rhs = S1[:, c * T + th * 512: c * T + (th + 1) * 512]
                    top(lambda e, o=PS[bank][:], l=lhs, r=rhs, st=(c == 0), sp=(c == CB - 1):
                        e.matmul(o, l, r, start=st, stop=sp),
                        inc=(j == 3 and th == 1))
            ws.consumed(sl, P.cur("tS"))
            if c + 3 < CB:
                _loadq(c + 3)
        q_done = P.cur("tS")
        exps = [0] * 8

        def _qstt(bank):
            j2, th2 = bank // 2, bank % 2
            h2 = grp * 4 + j2
            dstt = S3[:, h2 * T + th2 * 512: h2 * T + (th2 + 1) * 512]
            scr2 = FSCR[:, (bank % 2) * 512:(bank % 2) * 512 + 512]
            wait("vector", "sS", exps[bank])
            qb_free[bank] = vop(lambda e, o=dstt, p=PS[bank][:], x2=scr2:
                                e.scalar_tensor_tensor(o, p, 0.0, x2, ALU.max, ALU.add))
        wait("vector", "tS", q_done)
        for bank in range(8):
            scr = FSCR[:, (bank % 2) * 512:(bank % 2) * 512 + 512]
            v_min = vop(lambda e, o=scr, p=PS[bank][:]: e.tensor_scalar(o, p, 0.0, None, ALU.min))
            if bank >= 1:
                _qstt(bank - 1)
            wait("scalar", "vS", v_min)
            exps[bank] = sop(lambda e, o=scr, i_=scr: e.activation(out=o, in_=i_, func=AF.Exp))
        _qstt(7)
        tern1.emit(4)
    tern1.emit(CB)
    q_mm_all = P.cur("tS")
    q_evac_all = P.cur("vS")

    wait("vector", "dG", d_c1b)
    v_kvbf = vop(lambda e, o=KVSB[:, 0:2064], i_=KVP[:]: e.tensor_copy(o, i_))

    wait("tensor", "vS", v_kvbf)
    wait("tensor", "vS", max(qb_free))
    for h in range(H):
        for t in range(TB):
            lhs = S3[:, h * T + t * 128: h * T + (t + 1) * 128]
            col = t * 16 + h
            top(lambda e, o=PS[2][:, col:col + 1], l=lhs, r=KVSB[:, 2048 + h:2049 + h]:
                e.matmul(o, l, r, start=True, stop=True, skip_group_check=True),
                inc=(h == H - 1 and t == TB - 1))
    den_done = P.cur("tS")
    wait("vector", "tS", den_done)
    t1_ = vop(lambda e, o=ZTOK[:], p=PS[2][:, 0:128]: e.tensor_scalar(o, p, EPS_Z, None, ALU.add))
    wait("vector", "vS", t1_)
    t2_ = vop(lambda e, o=ZTOK[:], i_=ZTOK[:]: e.reciprocal(o, i_))
    wait("vector", "vS", t2_)
    t3_ = vop(lambda e, o=ZB[:, 0:128], i_=ZTOK[:]: e.tensor_copy(o, i_))
    wait("vector", "vS", t3_)
    v_zlo = vop(lambda e, o=ZB[:, 128:256], a=ZTOK[:], b=ZB[:, 0:128]:
                e.tensor_tensor(o, a, b, ALU.subtract))
    wait("tensor", "vS", v_zlo)
    top(lambda e, o=PT[0][:, 0:128], i_=ZB[:, 0:128], iden=IDB[:, 0:128]:
        e.transpose(o, i_, iden), inc=False)
    tr_z = top(lambda e, o=PT[0][:, 128:256], i_=ZB[:, 128:256], iden=IDB[:, 0:128]:
               e.transpose(o, i_, iden))
    wait("scalar", "tS", tr_z)
    sop(lambda e, o=ZB[:, 256:384], p=PT[0][:, 0:128]: e.copy(out=o, in_=p), inc=False)
    s_zrows = sop(lambda e, o=ZB[:, 384:512], p=PT[0][:, 128:256]: e.copy(out=o, in_=p))

    zhi_rows = ZB[:, 256:384]
    zlo_rows = ZB[:, 384:512]
    wait("tensor", "sS", s_zrows)
    wait("sync", "sS", s_zrows)
    pair_gate = [0, 0]
    zr_gate = 0
    for h in range(H):
        pair = h % 2
        zb0, zb1 = (4, 5) if pair == 0 else (2, 3)
        ob0 = 6 if pair == 0 else 0
        wait("sync", "tS", zr_gate)
        dma("sync", "dZ", ZR[0:1, 0:1024], zhi_rows[h:128:16, 0:128])
        d_zr = dma("sync", "dZ", ZR[0:1, 1024:2048], zlo_rows[h:128:16, 0:128])
        wait("tensor", "vS", pair_gate[pair])
        wait("tensor", "dZ", d_zr)
        for half, bank in ((0, zb0), (1, zb1)):
            top(lambda e, o=PS[bank][:], l=ONEB[0:1, 0:128],
                r=ZR[0:1, half * 512: (half + 1) * 512]:
                e.matmul(o, l, r, start=True, stop=False, skip_group_check=True), inc=False)
            top(lambda e, o=PS[bank][:], l=ONEB[0:1, 0:128],
                r=ZR[0:1, 1024 + half * 512: 1024 + (half + 1) * 512]:
                e.matmul(o, l, r, start=False, stop=True, skip_group_check=True),
                inc=(half == 1))
        zbc_done = P.cur("tS")
        zr_gate = zbc_done
        for th in range(2):
            top(lambda e, o=PS[ob0 + th][:], l=KVSB[:, h * 128:(h + 1) * 128],
                r=S3[:, h * T + th * 512: h * T + (th + 1) * 512]:
                e.matmul(o, l, r, start=True, stop=True), inc=(th == 1))
        out_done = P.cur("tS")
        wait("scalar", "tS", zbc_done)
        sop(lambda e, o=ZBC[:, 0:512], p=PS[zb0][:]: e.copy(out=o, in_=p), inc=False)
        s_zb = sop(lambda e, o=ZBC[:, 512:1024], p=PS[zb1][:]: e.copy(out=o, in_=p))
        wait("vector", "sS", s_zb)
        wait("vector", "tS", out_done)
        ev = 0
        for th in range(2):
            ev = vop(lambda e, o=S3[:, h * T + th * 512: h * T + (th + 1) * 512],
                     p=PS[ob0 + th][:], zz=ZBC[:, th * 512:(th + 1) * 512]:
                     e.tensor_tensor(o, p, zz, ALU.mult))
        pair_gate[pair] = ev
    out_phase_v = P.cur("vS")

    wait("sync", "vS", v_kvbf)
    d_xtok = [dma("sync", "dX", RREG[:, t * C:(t + 1) * C], x_tok_d[t * 128:(t + 1) * 128, :])
              for t in range(TB)]

    tern2 = Ternarizer(1, S1, q_mm_all)

    wait("tensor", "vS", out_phase_v)
    wait("tensor", "dX", d_xtok[-1])
    bank_free2 = [0] * 8
    for osl in range(4):
        tiles = {}

        def _loadw(c):
            tiles[c] = ws.load(wo_d[c * 128:(c + 1) * 128, osl * 512:(osl + 1) * 512])
        for c in range(3):
            _loadw(c)
        for c in range(CB):
            bf_ap, c_tgt, sl = tiles[c]
            wait("tensor", "sS", c_tgt)
            for t in range(TB):
                if c == 0:
                    wait("tensor", "vS", bank_free2[t])
                lhs = S3[:, c * T + t * 128: c * T + (t + 1) * 128]
                top(lambda e, o=PS[t][:], l=lhs, r=bf_ap, st=(c == 0), sp=(c == CB - 1):
                    e.matmul(o, l, r, start=st, stop=sp), inc=(t == TB - 1))
            ws.consumed(sl, P.cur("tS"))
            if c + 3 < CB:
                _loadw(c + 3)
        wo_done = P.cur("tS")
        for t in range(TB):
            wait("vector", "tS", wo_done)
            dst = RREG[:, t * C + osl * 512: t * C + (osl + 1) * 512]
            ev = vop(lambda e, o=dst, p=PS[t][:]:
                     e.scalar_tensor_tensor(o, p, 1.0, o, ALU.mult, ALU.add))
            bank_free2[t] = ev
        tern2.emit(4)
    tern2.emit(CB)
    wo_mm_all = P.cur("tS")
    tern_all = tern_done[-1]

    for t in range(TB):
        for g in range(4):
            vop(lambda e, o=BNS[:, g * 6:(g + 1) * 6],
                i_=RREG[:, t * C + g * 512: t * C + (g + 1) * 512]:
                e.bn_stats(o, i_), inc=(g == 3))
        bn_tgt = P.cur("vS")
        wait("vector", "vS", bn_tgt)
        for fn_ in (
            lambda e, t=t: e.bn_aggr(STA[:, t * 8:t * 8 + 2], BNS[:, 0:24]),
            lambda e, t=t: e.tensor_scalar(STA[:, t * 8 + 2:t * 8 + 3],
                                           STA[:, t * 8 + 1:t * 8 + 2], EPS_LN, None, ALU.add),
            lambda e, t=t: e.reciprocal(STA[:, t * 8 + 3:t * 8 + 4], STA[:, t * 8 + 2:t * 8 + 3]),
            lambda e, t=t: e.tensor_tensor(STA[:, t * 8 + 4:t * 8 + 5], STA[:, t * 8 + 1:t * 8 + 2],
                                           STA[:, t * 8 + 3:t * 8 + 4], ALU.mult),
            lambda e, t=t: e.tensor_scalar(STA[:, t * 8 + 4:t * 8 + 5], STA[:, t * 8 + 4:t * 8 + 5],
                                           EPS_RMS, None, ALU.add),
        ):
            v_m2 = vop(fn_)
            wait("vector", "vS", v_m2)
        wait("scalar", "vS", v_m2)
        sop(lambda e, o=STA[:, t * 8 + 2:t * 8 + 3], i_=STA[:, t * 8 + 2:t * 8 + 3]:
            e.activation(out=o, in_=i_, func=AF.Sqrt), inc=False)
        s_sq = sop(lambda e, o=STA[:, t * 8 + 4:t * 8 + 5], i_=STA[:, t * 8 + 4:t * 8 + 5]:
                   e.activation(out=o, in_=i_, func=AF.Sqrt))
        wait("vector", "sS", s_sq)
        t_ = vop(lambda e, o=STA[:, t * 8 + 2:t * 8 + 3], i_=STA[:, t * 8 + 2:t * 8 + 3]:
                 e.reciprocal(o, i_))
        wait("vector", "vS", t_)
        t_ = vop(lambda e, o=STA[:, t * 8 + 4:t * 8 + 5], i_=STA[:, t * 8 + 4:t * 8 + 5]:
                 e.reciprocal(o, i_))
        wait("vector", "vS", t_)
        vop(lambda e, o=RREG[:, t * C:(t + 1) * C], i_=RREG[:, t * C:(t + 1) * C],
            m=STA[:, t * 8:t * 8 + 1], s2=STA[:, t * 8 + 2:t * 8 + 3]:
            e.tensor_scalar(o, i_, m, s2, ALU.subtract, ALU.mult), inc=False)
        if t == 0:
            wait("vector", "dS", d_gam)
        v_xg = vop(lambda e, o=WFT[:, 0:2048], a=RREG[:, t * C:(t + 1) * C], b=GAM[:]:
                   e.tensor_tensor(o, a, b, ALU.mult))
        wait("vector", "vS", v_xg)
        t_ = vop(lambda e, o=STA[:, t * 8 + 5:t * 8 + 6], i_=WFT[:, 0:2048]:
                 e.tensor_reduce(o, i_, axis=AX, op=ALU.max, apply_absolute_value=True))
        wait("vector", "vS", t_)
        vop(lambda e, o=STA[:, t * 8 + 5:t * 8 + 6], a=STA[:, t * 8 + 5:t * 8 + 6],
            b=STA[:, t * 8 + 4:t * 8 + 5]: e.tensor_tensor(o, a, b, ALU.mult),
            inc=(t == TB - 1))
    v_cmb = vop(lambda e, o=COLS[:, 3:4], i_=STA[:, 5:6]: e.tensor_copy(o, i_))
    for t in range(1, TB):
        wait("vector", "vS", v_cmb)
        v_cmb = vop(lambda e, o=COLS[:, 3:4], a=COLS[:, 3:4], b=STA[:, t * 8 + 5:t * 8 + 6]:
                    e.tensor_tensor(o, a, b, ALU.max))
    wait("sync", "vS", v_cmb)
    d_g2 = dma("sync", "dZ", ROWS[0:1, 0:128], COLS[:, 3:4])

    wait("gpsimd", "dZ", d_g2)
    d_c2 = dma("gpsimd", "dG", cc2_in.ap(), ROWS[0:1, 0:128])
    wait("gpsimd", "dG", d_c2)
    cc2_tgt = P.inc("cc", 1)
    P.emit("gpsimd", lambda e: e.collective_compute(
        "AllReduce", ALU.max, replica_groups=[list(range(NCORES))],
        ins=[cc2_in.ap()], outs=[cc2_out.ap()]).then_inc(sems["cc"], 1))
    wait("gpsimd", "cc", cc2_tgt)
    d_c2b = dma("gpsimd", "dG", ROWS[0:1, 0:128], cc2_out.ap())
    wait("vector", "dG", d_c2b)
    v_s1 = vop(lambda e, o=ROWS[0:1, 131:132], i_=ROWS[0:1, 0:128]:
               e.tensor_reduce(o, i_, axis=AX, op=ALU.max))
    wait("vector", "vS", v_s1)
    for fn_ in (
        lambda e: e.tensor_scalar(ROWS[0:1, 131:132], ROWS[0:1, 131:132], EPS_SCALE, None, ALU.max),
        lambda e: e.reciprocal(ROWS[0:1, 132:133], ROWS[0:1, 131:132]),
        lambda e: e.tensor_scalar(ROWS[0:1, 133:134], ROWS[0:1, 132:133], 127.0, None, ALU.mult),
        lambda e: e.tensor_tensor(ROWS[0:1, 134:135], ROWS[0:1, 131:132], ROWS[0:1, 129:130], ALU.mult),
        lambda e: e.tensor_scalar(ROWS[0:1, 134:135], ROWS[0:1, 134:135], 1.0 / 127.0, None, ALU.mult),
        lambda e: e.tensor_scalar(ROWS[0:1, 256:384], ONESF[0:1, :], ROWS[0:1, 133:134], None, ALU.mult),
        lambda e: e.tensor_scalar(ROWS[0:1, 384:512], ONESF[0:1, :], ROWS[0:1, 134:135], None, ALU.mult),
    ):
        t_ = vop(fn_)
        wait("vector", "vS", t_)
    v_s7 = t_
    wait("sync", "vS", v_s7)
    dma("sync", "dZ", COLS[:, 4:5], ROWS[0:1, 256:384])
    d_sc3 = dma("sync", "dZ", COLS[:, 5:6], ROWS[0:1, 384:512])
    wait("vector", "dZ", d_sc3)
    b_done = 0
    for t in range(TB):
        b_done = vop(lambda e, o=STA[:, t * 8 + 6:t * 8 + 7], a=STA[:, t * 8 + 4:t * 8 + 5],
                     b=COLS[:, 4:5]: e.tensor_tensor(o, a, b, ALU.mult), inc=(t == TB - 1))

    wait("vector", "tS", wo_mm_all)
    wait("tensor", "vS", max(b_done, tern_all))
    ntok_free = [0, 0]
    nfm_free = [0, 0, 0]
    ffn_bank_free = [0] * 8
    tr_evac = {}
    last_store = 0
    for t in range(TB):
        tok_sl = t % 2
        fm_sl = t % 3
        tok_ap = S3[:, tok_sl * 2048:(tok_sl + 1) * 2048]
        fm_base = 4096 + fm_sl * 2048
        wait("vector", "tS", ntok_free[tok_sl])
        vop(lambda e, o=WFT[:, 0:2048], i_=RREG[:, t * C:(t + 1) * C],
            b=STA[:, t * 8 + 6:t * 8 + 7], g=GAM[:]:
            e.scalar_tensor_tensor(o, i_, b, g, ALU.mult, ALU.mult), inc=False)
        vop(lambda e, o=WFT[:, 0:2048], i_=WFT[:, 0:2048]:
            e.tensor_scalar(o, i_, -128.0, 127.0, ALU.max, ALU.min), inc=False)
        v_q = vop(lambda e, o=tok_ap, i_=WFT[:, 0:2048]:
                  e.tensor_scalar(o, i_, MAGIC, -MAGIC, ALU.add, ALU.add))
        wait("tensor", "vS", v_q)
        for c in range(CB):
            bank = c % 2
            off = ((c // 2) % 4) * 128
            if (t, c - 2) in tr_evac:
                wait("tensor", "sS", tr_evac[(t, c - 2)])
            elif c < 2 and t > 0 and (t - 1, 14 + c) in tr_evac:
                wait("tensor", "sS", tr_evac[(t - 1, 14 + c)])
            tr = top(lambda e, o=PT[bank][:, off:off + 128],
                     i_=tok_ap[:, c * 128:(c + 1) * 128], iden=IDB[:, 0:128]:
                     e.transpose(o, i_, iden))
            wait("scalar", "tS", tr)
            tr_evac[(t, c)] = sop(lambda e, o=S3[:, fm_base + c * 128: fm_base + (c + 1) * 128],
                                  p=PT[bank][:, off:off + 128]: e.copy(out=o, in_=p))
        ntok_free[tok_sl] = P.cur("tS")
        trans_s = tr_evac[(t, CB - 1)]
        wait("tensor", "sS", trans_s)
        pb = 0
        for osl in range(4):
            wait("tensor", "vS", ffn_bank_free[pb + osl])
        for c in range(CB):
            lhs = S3[:, fm_base + c * 128: fm_base + (c + 1) * 128]
            for osl in range(4):
                zone = S2 if osl < 2 else S1
                rhs = zone[:, c * 1024 + (osl % 2) * 512: c * 1024 + (osl % 2) * 512 + 512]
                top(lambda e, o=PS[pb + osl][:], l=lhs, r=rhs, st=(c == 0), sp=(c == CB - 1):
                    e.matmul(o, l, r, start=st, stop=sp),
                    inc=(c == CB - 1 and osl == 3))
        ffn_done = P.cur("tS")
        nfm_free[fm_sl] = ffn_done
        wait("vector", "tS", ffn_done)
        for osl in range(4):
            ev = vop(lambda e, o=RREG[:, t * C + osl * 512: t * C + (osl + 1) * 512],
                     p=PS[pb + osl][:], sc=COLS[:, 5:6]:
                     e.scalar_tensor_tensor(o, p, sc, o, ALU.mult, ALU.add))
            ffn_bank_free[pb + osl] = ev
        for g in range(4):
            vop(lambda e, o=BNS[:, g * 6:(g + 1) * 6],
                i_=RREG[:, t * C + g * 512: t * C + (g + 1) * 512]:
                e.bn_stats(o, i_), inc=(g == 3))
        bn2 = P.cur("vS")
        wait("vector", "vS", bn2)
        t_ = vop(lambda e, o=STA[:, t * 8 + 6:t * 8 + 8], i_=BNS[:, 0:24]: e.bn_aggr(o, i_))
        wait("vector", "vS", t_)
        v_u2 = vop(lambda e, o=STA[:, t * 8 + 7:t * 8 + 8], i_=STA[:, t * 8 + 7:t * 8 + 8]:
                   e.tensor_scalar(o, i_, EPS_LN, None, ALU.add))
        wait("scalar", "vS", v_u2)
        s_sq2 = sop(lambda e, o=STA[:, t * 8 + 7:t * 8 + 8], i_=STA[:, t * 8 + 7:t * 8 + 8]:
                    e.activation(out=o, in_=i_, func=AF.Sqrt))
        wait("vector", "sS", s_sq2)
        t_ = vop(lambda e, o=STA[:, t * 8 + 7:t * 8 + 8], i_=STA[:, t * 8 + 7:t * 8 + 8]:
                 e.reciprocal(o, i_))
        wait("vector", "vS", t_)
        v_out = vop(lambda e, o=RREG[:, t * C:(t + 1) * C], i_=RREG[:, t * C:(t + 1) * C],
                    m=STA[:, t * 8 + 6:t * 8 + 7], s2=STA[:, t * 8 + 7:t * 8 + 8]:
                    e.tensor_scalar(o, i_, m, s2, ALU.subtract, ALU.mult))
        wait("sync", "vS", v_out)
        last_store = dma("sync", "dS", y_d[t * 128:(t + 1) * 128, :], RREG[:, t * C:(t + 1) * C])

    wait("sync", "dS", last_store)
    wait("gpsimd", "dS", last_store)

    with es:
        block = es.enter_context(nc.Block())

        @block.sync
        def _(e):
            for f in P.ins["sync"]:
                f(e)

        @block.gpsimd
        def _(e):
            for f in P.ins["gpsimd"]:
                f(e)

        @block.tensor
        def _(e):
            for f in P.ins["tensor"]:
                f(e)

        @block.vector
        def _(e):
            for f in P.ins["vector"]:
                f(e)

        @block.scalar
        def _(e):
            for f in P.ins["scalar"]:
                f(e)

    return nc


def kernel(L0_attn_x, L0_attn_cos, L0_attn_sin, L0_attn_Wq, L0_attn_Wk, L0_attn_Wv,
           L0_attn_Wo, L0_ffn_weight, L0_ffn_norm_weight):
    x = np.ascontiguousarray(np.asarray(L0_attn_x, dtype=np.float32)[0])
    xT = np.ascontiguousarray(x.T)
    wq_t = np.ascontiguousarray(np.asarray(L0_attn_Wq, dtype=np.float32).T)
    wk_t = np.ascontiguousarray(np.asarray(L0_attn_Wk, dtype=np.float32).T)
    wv_t = np.ascontiguousarray(np.asarray(L0_attn_Wv, dtype=np.float32).T)
    wo_t = np.ascontiguousarray(np.asarray(L0_attn_Wo, dtype=np.float32).T)
    wf_t = np.ascontiguousarray(np.asarray(L0_ffn_weight, dtype=np.float32).T)
    gam = np.ascontiguousarray(np.broadcast_to(
        np.asarray(L0_ffn_norm_weight, dtype=np.float32)[None, :], (128, C)))
    iden = np.eye(128, dtype=np.float32)
    ones = np.ones((128, 128), dtype=np.float32)

    nc = build_nc()
    in_maps = []
    for cid in range(NCORES):
        in_maps.append({
            "x_fm": np.ascontiguousarray(xT[:, cid * T:(cid + 1) * T]),
            "x_tok": np.ascontiguousarray(x[cid * T:(cid + 1) * T, :]),
            "wq_t": wq_t, "wk_t": wk_t, "wv_t": wv_t, "wo_t": wo_t, "wf_t": wf_t,
            "gamma_bc": gam, "id_bf": iden, "ones_f": ones,
        })
    res = run_bass_kernel_spmd(nc, in_maps, core_ids=list(range(NCORES)))
    out = np.concatenate([res.results[i]["y"] for i in range(NCORES)], axis=0)
    return out.reshape(1, 8192, C)


# version 08
# speedup vs baseline: 1.1875x; 1.1228x over previous
import contextlib
import numpy as np
import concourse.bass as bass
import concourse.mybir as mybir
from concourse.bass_utils import run_bass_kernel_spmd

F32 = mybir.dt.float32
BF16 = mybir.dt.bfloat16
AF = mybir.ActivationFunctionType
ALU = mybir.AluOpType
AX = mybir.AxisListType.X

NCORES = 8
T = 1024
TB = 8
C = 2048
CB = 16
H = 16
MAGIC = 12582912.0
EPS_LN = 1e-5
EPS_RMS = 1e-6
EPS_SCALE = 1e-5
EPS_Z = 1e-6


class Prog:
    ENGINES = ("sync", "gpsimd", "tensor", "vector", "scalar")

    def __init__(self):
        self.ins = {e: [] for e in self.ENGINES}
        self.sem_ct = {}

    def emit(self, eng, fn):
        self.ins[eng].append(fn)

    def inc(self, sem, amount):
        self.sem_ct[sem] = self.sem_ct.get(sem, 0) + amount
        return self.sem_ct[sem]

    def cur(self, sem):
        return self.sem_ct.get(sem, 0)


def build_nc():
    nc = bass.Bass("TRN2", target_bir_lowering=False, debug=False)

    x_fm_d = nc.dram_tensor("x_fm", [C, T], F32, kind="ExternalInput").ap()
    x_tok_d = nc.dram_tensor("x_tok", [T, C], F32, kind="ExternalInput").ap()
    wq_d = nc.dram_tensor("wq_t", [C, C], F32, kind="ExternalInput").ap()
    wk_d = nc.dram_tensor("wk_t", [C, C], F32, kind="ExternalInput").ap()
    wv_d = nc.dram_tensor("wv_t", [C, C], F32, kind="ExternalInput").ap()
    wo_d = nc.dram_tensor("wo_t", [C, C], F32, kind="ExternalInput").ap()
    wf_d = nc.dram_tensor("wf_t", [C, C], F32, kind="ExternalInput").ap()
    gam_d = nc.dram_tensor("gamma_bc", [128, C], F32, kind="ExternalInput").ap()
    id_d = nc.dram_tensor("id_bf", [128, 128], F32, kind="ExternalInput").ap()
    ones_d = nc.dram_tensor("ones_f", [128, 128], F32, kind="ExternalInput").ap()
    y_d = nc.dram_tensor("y", [T, C], F32, kind="ExternalOutput").ap()

    cc1_in = nc.dram_tensor("cc1_in", [128, 2064], F32)
    cc1_out = nc.dram_tensor("cc1_out", [128, 2064], F32)
    cc2_in = nc.dram_tensor("cc2_in", [1, 128], F32)
    cc2_out = nc.dram_tensor("cc2_out", [1, 128], F32)

    P = Prog()
    es = contextlib.ExitStack()

    def sb(name, shape, dtype=F32):
        return es.enter_context(nc.sbuf_tensor(name, shape, dtype))

    RREG = sb("RREG", [128, TB * C])
    GAM = sb("GAM", [128, C])
    WFT = sb("WFT", [128, C])
    FSCR = sb("FSCR", [128, 1024])
    WST = sb("WST", [128, 1536])
    S1f = sb("S1f", [128, 8192])
    S2f = sb("S2f", [128, 8192])
    S3f = sb("S3f", [128, 8192])
    S1 = S1f.bitcast(BF16)
    S2 = S2f.bitcast(BF16)
    S3 = S3f.bitcast(BF16)
    WSTB = sb("WSTB", [128, 768]).bitcast(BF16)
    KVSB = sb("KVSB", [128, 1032]).bitcast(BF16)
    ZB = sb("ZB", [128, 256]).bitcast(BF16)
    IDB = sb("IDB", [128, 64]).bitcast(BF16)
    ONEB = sb("ONEB", [128, 64]).bitcast(BF16)
    ZBC = sb("ZBC", [128, 512])
    ZTOK = sb("ZTOK", [128, 128])
    BNS = sb("BNS", [128, 24])
    STA = sb("STA", [128, 64])
    ACC = sb("ACC", [128, 32])
    COLS = sb("COLS", [128, 8])
    ROWS = sb("ROWS", [1, 396])
    ZR2 = sb("ZR2", [1, 1024]).bitcast(BF16)
    ZR = sb("ZR", [1, 1024]).bitcast(BF16)
    ONESF = sb("ONESF", [1, 128])

    KVP = RREG[:, 1024:3088]

    PS = [es.enter_context(nc.psum_tensor(f"ps{i}", [128, 512], F32)) for i in range(6)]
    PT = [es.enter_context(nc.psum_tensor(f"pt{i}", [128, 1024], BF16)) for i in range(2)]
    PS.append(PT[0].bitcast(F32))
    PS.append(PT[1].bitcast(F32))

    sems = {}
    for s in ("dS", "dG", "dC", "tS", "vS", "sS", "cc",
              "dW0", "dW1", "dW2", "dF0", "dF1", "dX", "dZ"):
        sems[s] = es.enter_context(nc.semaphore(s))

    def dma(eng, qsem, out, in_):
        tgt = P.inc(qsem, 16)
        P.emit(eng, lambda e, o=out, i_=in_, s=sems[qsem]: e.dma_start(out=o, in_=i_).then_inc(s, 16))
        return tgt

    def wait(eng, sem, tgt):
        if tgt and tgt > 0:
            P.emit(eng, lambda e, s=sems[sem], t=tgt: e.wait_ge(s, t))

    def vop(fn, inc=True):
        if inc:
            tgt = P.inc("vS", 1)
            P.emit("vector", lambda e, f=fn, s=sems["vS"]: f(e).then_inc(s, 1))
            return tgt
        P.emit("vector", lambda e, f=fn: f(e))
        return P.cur("vS")

    def sop(fn, inc=True):
        if inc:
            tgt = P.inc("sS", 1)
            P.emit("scalar", lambda e, f=fn, s=sems["sS"]: f(e).then_inc(s, 1))
            return tgt
        P.emit("scalar", lambda e, f=fn: f(e))
        return P.cur("sS")

    def top(fn, inc=True):
        if inc:
            tgt = P.inc("tS", 1)
            P.emit("tensor", lambda e, f=fn, s=sems["tS"]: f(e).then_inc(s, 1))
            return tgt
        P.emit("tensor", lambda e, f=fn: f(e))
        return P.cur("tS")

    d_id = dma("gpsimd", "dG", IDB[:, 0:128], id_d)
    d_ones = dma("gpsimd", "dG", ONEB[:, 0:128], ones_d)
    d_onesf = dma("sync", "dS", ONESF[0:1, :], ones_d[0:1, :])
    d_gam = dma("sync", "dS", GAM[:], gam_d)
    d_xfm_last = 0
    for c in range(CB):
        d_xfm_last = dma("gpsimd", "dG", S1[:, c * T:(c + 1) * T], x_fm_d[c * 128:(c + 1) * 128, :])

    class WStream:
        def __init__(self):
            self.i = 0
            self.f32_free = [0, 0, 0]
            self.bf_free = [0, 0, 0]

        def load(self, src_ap):
            sl = self.i % 3
            self.i += 1
            f32_ap = WST[:, sl * 512:(sl + 1) * 512]
            bf_ap = WSTB[:, sl * 512:(sl + 1) * 512]
            wait("sync", "sS", self.f32_free[sl])
            d_tgt = dma("sync", f"dW{sl}", f32_ap, src_ap)
            wait("scalar", f"dW{sl}", d_tgt)
            wait("scalar", "tS", self.bf_free[sl])
            c_tgt = sop(lambda e, o=bf_ap, i_=f32_ap: e.copy(out=o, in_=i_))
            self.f32_free[sl] = c_tgt
            return bf_ap, c_tgt, sl

        def consumed(self, sl, t_tgt):
            self.bf_free[sl] = t_tgt

    ws = WStream()

    wait("tensor", "dG", d_xfm_last)
    bank_free = [0] * 8

    def proj_slice(osl, which):
        wsrc = wk_d if which == "k" else wv_d
        base = 0 if which == "k" else 4096
        tiles = {}

        def _load(c):
            tiles[c] = ws.load(wsrc[c * 128:(c + 1) * 128, osl * 512:(osl + 1) * 512])
        for c in range(3):
            _load(c)
        for c in range(CB):
            bf_ap, c_tgt, sl = tiles[c]
            wait("tensor", "sS", c_tgt)
            for t in range(TB):
                if c == 0:
                    wait("tensor", "vS", bank_free[t])
                lhs = S1[:, c * T + t * 128: c * T + (t + 1) * 128]
                top(lambda e, o=PS[t][:], l=lhs, r=bf_ap, st=(c == 0), sp=(c == CB - 1):
                    e.matmul(o, l, r, start=st, stop=sp), inc=(t == TB - 1))
            ws.consumed(sl, P.cur("tS"))
            if c + 3 < CB:
                _load(c + 3)
        mm_done = P.cur("tS")
        evac = [0] * TB
        if which == "k":
            exps = [0] * TB

            def _stt(t):
                wait("vector", "sS", exps[t])
                dstt = S2[:, base + t * 512: base + (t + 1) * 512]
                scr2 = FSCR[:, (t % 2) * 512:(t % 2) * 512 + 512]
                evac[t] = vop(lambda e, o=dstt, p=PS[t][:], x2=scr2:
                              e.scalar_tensor_tensor(o, p, 0.0, x2, ALU.max, ALU.add))
                bank_free[t] = evac[t]
            wait("vector", "tS", mm_done)
            for t in range(TB):
                scr = FSCR[:, (t % 2) * 512:(t % 2) * 512 + 512]
                v_min = vop(lambda e, o=scr, p=PS[t][:]: e.tensor_scalar(o, p, 0.0, None, ALU.min))
                if t >= 1:
                    _stt(t - 1)
                wait("scalar", "vS", v_min)
                exps[t] = sop(lambda e, o=scr, i_=scr: e.activation(out=o, in_=i_, func=AF.Exp))
            _stt(TB - 1)
        else:
            wait("vector", "tS", mm_done)
            for t in range(TB):
                dst = S2[:, base + t * 512: base + (t + 1) * 512]
                ev = vop(lambda e, o=dst, p=PS[t][:]: e.tensor_copy(o, p))
                evac[t] = ev
                bank_free[t] = ev
        return evac

    class Absum:
        def __init__(self):
            self.di = 0
            self.ci = 0
            self.dtgt = {}
            self.rtgt = {}

        def _dma_one(self):
            i = self.di
            self.di += 1
            ch, hf = i // 2, i % 2
            if i >= 2:
                wait("gpsimd", "vS", self.rtgt[i - 2])
            self.dtgt[i] = dma("gpsimd", f"dF{i % 2}", WFT[:, (i % 2) * 1024:(i % 2) * 1024 + 1024],
                               wf_d[ch * 128:(ch + 1) * 128, hf * 1024:(hf + 1) * 1024])

        def emit(self, n):
            for _ in range(n):
                if self.ci >= 32:
                    return
                while self.di < min(self.ci + 2, 32):
                    self._dma_one()
                i = self.ci
                self.ci += 1
                wait("vector", f"dF{i % 2}", self.dtgt[i])
                self.rtgt[i] = vop(lambda e, o=ACC[:, i:i + 1],
                                   i_=WFT[:, (i % 2) * 1024:(i % 2) * 1024 + 1024]:
                                   e.tensor_reduce(o, i_, axis=AX, op=ALU.add,
                                                   apply_absolute_value=True))
                while self.di < min(self.ci + 2, 32):
                    self._dma_one()

    absum = Absum()

    kv_evac_last = 0
    for osl in range(4):
        ek = proj_slice(osl, "k")
        absum.emit(4)
        ev = proj_slice(osl, "v")
        absum.emit(4)
        wait("tensor", "vS", bank_free[0])
        wait("tensor", "vS", bank_free[1])
        wait("tensor", "vS", max(ek[-1], ev[-1]))
        for t in range(TB):
            for hh in range(4):
                lhs = S2[:, t * 512 + hh * 128: t * 512 + (hh + 1) * 128]
                rhs = S2[:, 4096 + t * 512 + hh * 128: 4096 + t * 512 + (hh + 1) * 128]
                top(lambda e, o=PS[0][:, hh * 128:(hh + 1) * 128], l=lhs, r=rhs,
                    st=(t == 0 and hh == 0), sp=(t == TB - 1 and hh == 3):
                    e.matmul(o, l, r, start=st, stop=sp, skip_group_check=True), inc=False)
                top(lambda e, o=PS[1][:, hh:hh + 1], l=lhs, r=ONEB[:, 0:1],
                    st=(t == 0 and hh == 0), sp=(t == TB - 1 and hh == 3):
                    e.matmul(o, l, r, start=st, stop=sp, skip_group_check=True),
                    inc=(t == TB - 1 and hh == 3))
        kv_done_t = P.cur("tS")
        wait("vector", "tS", kv_done_t)
        vop(lambda e, o=KVP[:, osl * 512:(osl + 1) * 512], p=PS[0][:]: e.tensor_copy(o, p), inc=False)
        kv_evac_last = vop(lambda e, o=KVP[:, 2048 + osl * 4: 2048 + (osl + 1) * 4],
                           p=PS[1][:, 0:4]: e.tensor_copy(o, p))
        bank_free[0] = kv_evac_last
        bank_free[1] = kv_evac_last
    kv_mm_all = P.cur("tS")
    absum.emit(32)
    absum_last = absum.rtgt[31]

    wait("vector", "vS", absum_last)
    v_asum = vop(lambda e, o=COLS[:, 2:3], i_=ACC[:, 0:32]: e.tensor_reduce(o, i_, axis=AX, op=ALU.add))
    wait("vector", "vS", v_asum)
    wait("sync", "vS", v_asum)
    d_g1 = dma("sync", "dZ", ROWS[0:1, 0:128], COLS[:, 2:3])
    wait("vector", "dZ", d_g1)
    v_w1 = vop(lambda e, o=ROWS[0:1, 128:129], i_=ROWS[0:1, 0:128]:
               e.tensor_reduce(o, i_, axis=AX, op=ALU.add))
    wait("vector", "vS", v_w1)
    t1_ = vop(lambda e, o=ROWS[0:1, 129:130], i_=ROWS[0:1, 128:129]:
        e.tensor_scalar(o, i_, 1.0 / (C * C), EPS_SCALE, ALU.mult, ALU.max))
    wait("vector", "vS", t1_)
    t2_ = vop(lambda e, o=ROWS[0:1, 130:131], i_=ROWS[0:1, 129:130]: e.reciprocal(o, i_))
    wait("vector", "vS", t2_)
    wait("vector", "dS", d_onesf)
    v_w4 = vop(lambda e, o=ROWS[0:1, 140:268], i_=ONESF[0:1, :]:
               e.tensor_scalar(o, i_, ROWS[0:1, 130:131], None, ALU.mult))
    wait("sync", "vS", v_w4)
    d_invws = dma("sync", "dZ", COLS[:, 0:1], ROWS[0:1, 140:268])

    wait("gpsimd", "vS", kv_evac_last)
    d_c1 = dma("gpsimd", "dG", cc1_in.ap(), KVP[:])
    wait("gpsimd", "dG", d_c1)
    cc1_tgt = P.inc("cc", 1)
    P.emit("gpsimd", lambda e: e.collective_compute(
        "AllReduce", ALU.add, replica_groups=[list(range(NCORES))],
        ins=[cc1_in.ap()], outs=[cc1_out.ap()]).then_inc(sems["cc"], 1))
    wait("gpsimd", "cc", cc1_tgt)
    d_c1b = dma("gpsimd", "dG", KVP[:], cc1_out.ap())

    tern_done = []

    class Ternarizer:
        def __init__(self, half, zone, tens_gate):
            self.half, self.zone, self.gate = half, zone, tens_gate
            self.di = 0
            self.ci = 0
            self.dtgt = {}
            self.ctgt = {}

        def _dma_one(self):
            ch = self.di
            self.di += 1
            if ch < 2 and self.half == 0:
                wait("gpsimd", "vS", absum_last)
            if ch >= 2:
                wait("gpsimd", "vS", self.ctgt[ch - 2])
            elif self.half == 1 and ch < 2:
                wait("gpsimd", "vS", tern_done[CB - 2 + ch])
            self.dtgt[ch] = dma("gpsimd", f"dF{ch % 2}", WFT[:, (ch % 2) * 1024:(ch % 2) * 1024 + 1024],
                                wf_d[ch * 128:(ch + 1) * 128,
                                     self.half * 1024:(self.half + 1) * 1024])

        def emit(self, n):
            for _ in range(n):
                if self.ci >= CB:
                    return
                while self.di < min(self.ci + 2, CB):
                    self._dma_one()
                ch = self.ci
                self.ci += 1
                f32_ap = WFT[:, (ch % 2) * 1024:(ch % 2) * 1024 + 1024]
                wait("vector", f"dF{ch % 2}", self.dtgt[ch])
                if ch == 0:
                    wait("vector", "dZ", d_invws)
                    wait("vector", "tS", self.gate)
                dst = self.zone[:, ch * 1024:(ch + 1) * 1024]
                vop(lambda e, o=f32_ap, i_=f32_ap:
                    e.tensor_scalar(o, i_, COLS[:, 0:1], MAGIC, ALU.mult, ALU.add), inc=False)
                vop(lambda e, o=f32_ap, i_=f32_ap:
                    e.tensor_scalar(o, i_, -MAGIC, -1.0, ALU.add, ALU.max), inc=False)
                t3 = vop(lambda e, o=dst, i_=f32_ap: e.tensor_scalar(o, i_, 1.0, None, ALU.min))
                self.ctgt[ch] = t3
                tern_done.append(t3)
                while self.di < min(self.ci + 2, CB):
                    self._dma_one()

    tern1 = Ternarizer(0, S2, kv_mm_all)
    qb_free = list(bank_free)
    for grp in range(4):
        tiles = {}

        def _loadq(c):
            tiles[c] = ws.load(wq_d[c * 128:(c + 1) * 128, grp * 512:(grp + 1) * 512])
        for c in range(3):
            _loadq(c)
        for c in range(CB):
            bf_ap, c_tgt, sl = tiles[c]
            wait("tensor", "sS", c_tgt)
            for j in range(4):
                for th in range(2):
                    bank = 2 * j + th
                    if c == 0:
                        wait("tensor", "vS", qb_free[bank])
                    lhs = bf_ap[:, j * 128:(j + 1) * 128]
                    rhs = S1[:, c * T + th * 512: c * T + (th + 1) * 512]
                    top(lambda e, o=PS[bank][:], l=lhs, r=rhs, st=(c == 0), sp=(c == CB - 1):
                        e.matmul(o, l, r, start=st, stop=sp),
                        inc=(j == 3 and th == 1))
            ws.consumed(sl, P.cur("tS"))
            if c + 3 < CB:
                _loadq(c + 3)
        q_done = P.cur("tS")
        exps = [0] * 8

        def _qstt(bank):
            j2, th2 = bank // 2, bank % 2
            h2 = grp * 4 + j2
            dstt = S3[:, h2 * T + th2 * 512: h2 * T + (th2 + 1) * 512]
            scr2 = FSCR[:, (bank % 2) * 512:(bank % 2) * 512 + 512]
            wait("vector", "sS", exps[bank])
            qb_free[bank] = vop(lambda e, o=dstt, p=PS[bank][:], x2=scr2:
                                e.scalar_tensor_tensor(o, p, 0.0, x2, ALU.max, ALU.add))
        wait("vector", "tS", q_done)
        for bank in range(8):
            scr = FSCR[:, (bank % 2) * 512:(bank % 2) * 512 + 512]
            v_min = vop(lambda e, o=scr, p=PS[bank][:]: e.tensor_scalar(o, p, 0.0, None, ALU.min))
            if bank >= 1:
                _qstt(bank - 1)
            wait("scalar", "vS", v_min)
            exps[bank] = sop(lambda e, o=scr, i_=scr: e.activation(out=o, in_=i_, func=AF.Exp))
        _qstt(7)
        tern1.emit(4)
    tern1.emit(CB)
    q_mm_all = P.cur("tS")
    q_evac_all = P.cur("vS")

    wait("vector", "dG", d_c1b)
    v_kvbf = vop(lambda e, o=KVSB[:, 0:2064], i_=KVP[:]: e.tensor_copy(o, i_))

    wait("tensor", "vS", v_kvbf)
    wait("tensor", "vS", max(qb_free))
    for h in range(H):
        for t in range(TB):
            lhs = S3[:, h * T + t * 128: h * T + (t + 1) * 128]
            col = t * 16 + h
            top(lambda e, o=PS[2][:, col:col + 1], l=lhs, r=KVSB[:, 2048 + h:2049 + h]:
                e.matmul(o, l, r, start=True, stop=True, skip_group_check=True),
                inc=(h == H - 1 and t == TB - 1))
    den_done = P.cur("tS")
    wait("vector", "tS", den_done)
    t1_ = vop(lambda e, o=ZTOK[:], p=PS[2][:, 0:128]: e.tensor_scalar(o, p, EPS_Z, None, ALU.add))
    wait("vector", "vS", t1_)
    t2_ = vop(lambda e, o=ZTOK[:], i_=ZTOK[:]: e.reciprocal(o, i_))
    wait("vector", "vS", t2_)
    t3_ = vop(lambda e, o=ZB[:, 0:128], i_=ZTOK[:]: e.tensor_copy(o, i_))
    wait("vector", "vS", t3_)
    v_zlo = vop(lambda e, o=ZB[:, 128:256], a=ZTOK[:], b=ZB[:, 0:128]:
                e.tensor_tensor(o, a, b, ALU.subtract))
    wait("tensor", "vS", v_zlo)
    top(lambda e, o=PT[0][:, 0:128], i_=ZB[:, 0:128], iden=IDB[:, 0:128]:
        e.transpose(o, i_, iden), inc=False)
    tr_z = top(lambda e, o=PT[0][:, 128:256], i_=ZB[:, 128:256], iden=IDB[:, 0:128]:
               e.transpose(o, i_, iden))
    wait("scalar", "tS", tr_z)
    sop(lambda e, o=ZB[:, 256:384], p=PT[0][:, 0:128]: e.copy(out=o, in_=p), inc=False)
    s_zrows = sop(lambda e, o=ZB[:, 384:512], p=PT[0][:, 128:256]: e.copy(out=o, in_=p))

    zhi_rows = ZB[:, 256:384]
    zlo_rows = ZB[:, 384:512]
    wait("tensor", "sS", s_zrows)
    wait("sync", "sS", s_zrows)
    wait("scalar", "sS", s_zrows)
    pair_gate = [0, 0]
    zr_gate = [0, 0]
    d_zr = {}

    def _gather(h):
        buf = ZR if h % 2 == 0 else ZR2
        wait("sync", "tS", zr_gate[h % 2])
        wait("scalar", "tS", zr_gate[h % 2])
        dma("sync", "dZ", buf[0:1, 0:1024], zhi_rows[h:128:16, 0:128])
        d_lo = dma("scalar", "dC", buf[0:1, 1024:2048], zlo_rows[h:128:16, 0:128])
        d_zr[h] = (P.cur("dZ"), d_lo)
    _gather(0)
    _gather(1)
    for h in range(H):
        pair = h % 2
        zb0, zb1 = (4, 5) if pair == 0 else (2, 3)
        ob0 = 6 if pair == 0 else 0
        buf = ZR if h % 2 == 0 else ZR2
        wait("tensor", "vS", pair_gate[pair])
        wait("tensor", "dZ", d_zr[h][0])
        wait("tensor", "dC", d_zr[h][1])
        for half, bank in ((0, zb0), (1, zb1)):
            top(lambda e, o=PS[bank][:], l=ONEB[0:1, 0:128],
                r=buf[0:1, half * 512: (half + 1) * 512]:
                e.matmul(o, l, r, start=True, stop=False, skip_group_check=True), inc=False)
            top(lambda e, o=PS[bank][:], l=ONEB[0:1, 0:128],
                r=buf[0:1, 1024 + half * 512: 1024 + (half + 1) * 512]:
                e.matmul(o, l, r, start=False, stop=True, skip_group_check=True),
                inc=(half == 1))
        zbc_done = P.cur("tS")
        zr_gate[pair] = zbc_done
        if h + 2 < H:
            _gather(h + 2)
        for th in range(2):
            top(lambda e, o=PS[ob0 + th][:], l=KVSB[:, h * 128:(h + 1) * 128],
                r=S3[:, h * T + th * 512: h * T + (th + 1) * 512]:
                e.matmul(o, l, r, start=True, stop=True), inc=(th == 1))
        out_done = P.cur("tS")
        wait("vector", "tS", out_done)
        ev = 0
        for th, zb in ((0, zb0), (1, zb1)):
            wait("scalar", "tS", zbc_done)
            s_zb = sop(lambda e, o=ZBC[:], p=PS[zb][:]: e.copy(out=o, in_=p))
            wait("vector", "sS", s_zb)
            ev = vop(lambda e, o=S3[:, h * T + th * 512: h * T + (th + 1) * 512],
                     p=PS[ob0 + th][:], zz=ZBC[:]:
                     e.tensor_tensor(o, p, zz, ALU.mult))
        pair_gate[pair] = ev
    out_phase_v = P.cur("vS")

    wait("sync", "vS", v_kvbf)
    d_xtok = [dma("sync", "dX", RREG[:, t * C:(t + 1) * C], x_tok_d[t * 128:(t + 1) * 128, :])
              for t in range(TB)]

    tern2 = Ternarizer(1, S1, q_mm_all)

    wait("tensor", "vS", out_phase_v)
    wait("tensor", "dX", d_xtok[-1])
    bank_free2 = [0] * 8
    for osl in range(4):
        tiles = {}

        def _loadw(c):
            tiles[c] = ws.load(wo_d[c * 128:(c + 1) * 128, osl * 512:(osl + 1) * 512])
        for c in range(3):
            _loadw(c)
        for c in range(CB):
            bf_ap, c_tgt, sl = tiles[c]
            wait("tensor", "sS", c_tgt)
            for t in range(TB):
                if c == 0:
                    wait("tensor", "vS", bank_free2[t])
                lhs = S3[:, c * T + t * 128: c * T + (t + 1) * 128]
                top(lambda e, o=PS[t][:], l=lhs, r=bf_ap, st=(c == 0), sp=(c == CB - 1):
                    e.matmul(o, l, r, start=st, stop=sp), inc=(t == TB - 1))
            ws.consumed(sl, P.cur("tS"))
            if c + 3 < CB:
                _loadw(c + 3)
        wo_done = P.cur("tS")
        for t in range(TB):
            wait("vector", "tS", wo_done)
            dst = RREG[:, t * C + osl * 512: t * C + (osl + 1) * 512]
            ev = vop(lambda e, o=dst, p=PS[t][:]:
                     e.scalar_tensor_tensor(o, p, 1.0, o, ALU.mult, ALU.add))
            bank_free2[t] = ev
        tern2.emit(4)
    tern2.emit(CB)
    wo_mm_all = P.cur("tS")
    tern_all = tern_done[-1]

    for t in range(TB):
        for g in range(4):
            vop(lambda e, o=BNS[:, g * 6:(g + 1) * 6],
                i_=RREG[:, t * C + g * 512: t * C + (g + 1) * 512]:
                e.bn_stats(o, i_), inc=(g == 3))
        bn_tgt = P.cur("vS")
        wait("vector", "vS", bn_tgt)
        for fn_ in (
            lambda e, t=t: e.bn_aggr(STA[:, t * 8:t * 8 + 2], BNS[:, 0:24]),
            lambda e, t=t: e.tensor_scalar(STA[:, t * 8 + 2:t * 8 + 3],
                                           STA[:, t * 8 + 1:t * 8 + 2], EPS_LN, None, ALU.add),
            lambda e, t=t: e.reciprocal(STA[:, t * 8 + 3:t * 8 + 4], STA[:, t * 8 + 2:t * 8 + 3]),
            lambda e, t=t: e.tensor_tensor(STA[:, t * 8 + 4:t * 8 + 5], STA[:, t * 8 + 1:t * 8 + 2],
                                           STA[:, t * 8 + 3:t * 8 + 4], ALU.mult),
            lambda e, t=t: e.tensor_scalar(STA[:, t * 8 + 4:t * 8 + 5], STA[:, t * 8 + 4:t * 8 + 5],
                                           EPS_RMS, None, ALU.add),
        ):
            v_m2 = vop(fn_)
            wait("vector", "vS", v_m2)
        wait("scalar", "vS", v_m2)
        sop(lambda e, o=STA[:, t * 8 + 2:t * 8 + 3], i_=STA[:, t * 8 + 2:t * 8 + 3]:
            e.activation(out=o, in_=i_, func=AF.Sqrt), inc=False)
        s_sq = sop(lambda e, o=STA[:, t * 8 + 4:t * 8 + 5], i_=STA[:, t * 8 + 4:t * 8 + 5]:
                   e.activation(out=o, in_=i_, func=AF.Sqrt))
        wait("vector", "sS", s_sq)
        t_ = vop(lambda e, o=STA[:, t * 8 + 2:t * 8 + 3], i_=STA[:, t * 8 + 2:t * 8 + 3]:
                 e.reciprocal(o, i_))
        wait("vector", "vS", t_)
        t_ = vop(lambda e, o=STA[:, t * 8 + 4:t * 8 + 5], i_=STA[:, t * 8 + 4:t * 8 + 5]:
                 e.reciprocal(o, i_))
        wait("vector", "vS", t_)
        vop(lambda e, o=RREG[:, t * C:(t + 1) * C], i_=RREG[:, t * C:(t + 1) * C],
            m=STA[:, t * 8:t * 8 + 1], s2=STA[:, t * 8 + 2:t * 8 + 3]:
            e.tensor_scalar(o, i_, m, s2, ALU.subtract, ALU.mult), inc=False)
        if t == 0:
            wait("vector", "dS", d_gam)
        v_xg = vop(lambda e, o=WFT[:, 0:2048], a=RREG[:, t * C:(t + 1) * C], b=GAM[:]:
                   e.tensor_tensor(o, a, b, ALU.mult))
        wait("vector", "vS", v_xg)
        t_ = vop(lambda e, o=STA[:, t * 8 + 5:t * 8 + 6], i_=WFT[:, 0:2048]:
                 e.tensor_reduce(o, i_, axis=AX, op=ALU.max, apply_absolute_value=True))
        wait("vector", "vS", t_)
        vop(lambda e, o=STA[:, t * 8 + 5:t * 8 + 6], a=STA[:, t * 8 + 5:t * 8 + 6],
            b=STA[:, t * 8 + 4:t * 8 + 5]: e.tensor_tensor(o, a, b, ALU.mult),
            inc=(t == TB - 1))
    v_cmb = vop(lambda e, o=COLS[:, 3:4], i_=STA[:, 5:6]: e.tensor_copy(o, i_))
    for t in range(1, TB):
        wait("vector", "vS", v_cmb)
        v_cmb = vop(lambda e, o=COLS[:, 3:4], a=COLS[:, 3:4], b=STA[:, t * 8 + 5:t * 8 + 6]:
                    e.tensor_tensor(o, a, b, ALU.max))
    wait("sync", "vS", v_cmb)
    d_g2 = dma("sync", "dZ", ROWS[0:1, 0:128], COLS[:, 3:4])

    wait("gpsimd", "dZ", d_g2)
    d_c2 = dma("gpsimd", "dG", cc2_in.ap(), ROWS[0:1, 0:128])
    wait("gpsimd", "dG", d_c2)
    cc2_tgt = P.inc("cc", 1)
    P.emit("gpsimd", lambda e: e.collective_compute(
        "AllReduce", ALU.max, replica_groups=[list(range(NCORES))],
        ins=[cc2_in.ap()], outs=[cc2_out.ap()]).then_inc(sems["cc"], 1))
    wait("gpsimd", "cc", cc2_tgt)
    d_c2b = dma("gpsimd", "dG", ROWS[0:1, 0:128], cc2_out.ap())
    wait("vector", "dG", d_c2b)
    v_s1 = vop(lambda e, o=ROWS[0:1, 131:132], i_=ROWS[0:1, 0:128]:
               e.tensor_reduce(o, i_, axis=AX, op=ALU.max))
    wait("vector", "vS", v_s1)
    for fn_ in (
        lambda e: e.tensor_scalar(ROWS[0:1, 131:132], ROWS[0:1, 131:132], EPS_SCALE, None, ALU.max),
        lambda e: e.reciprocal(ROWS[0:1, 132:133], ROWS[0:1, 131:132]),
        lambda e: e.tensor_scalar(ROWS[0:1, 133:134], ROWS[0:1, 132:133], 127.0, None, ALU.mult),
        lambda e: e.tensor_tensor(ROWS[0:1, 134:135], ROWS[0:1, 131:132], ROWS[0:1, 129:130], ALU.mult),
        lambda e: e.tensor_scalar(ROWS[0:1, 134:135], ROWS[0:1, 134:135], 1.0 / 127.0, None, ALU.mult),
        lambda e: e.tensor_scalar(ROWS[0:1, 140:268], ONESF[0:1, :], ROWS[0:1, 133:134], None, ALU.mult),
        lambda e: e.tensor_scalar(ROWS[0:1, 268:396], ONESF[0:1, :], ROWS[0:1, 134:135], None, ALU.mult),
    ):
        t_ = vop(fn_)
        wait("vector", "vS", t_)
    v_s7 = t_
    wait("sync", "vS", v_s7)
    dma("sync", "dZ", COLS[:, 4:5], ROWS[0:1, 140:268])
    d_sc3 = dma("sync", "dZ", COLS[:, 5:6], ROWS[0:1, 268:396])
    wait("vector", "dZ", d_sc3)
    b_done = 0
    for t in range(TB):
        b_done = vop(lambda e, o=STA[:, t * 8 + 6:t * 8 + 7], a=STA[:, t * 8 + 4:t * 8 + 5],
                     b=COLS[:, 4:5]: e.tensor_tensor(o, a, b, ALU.mult), inc=(t == TB - 1))

    wait("vector", "tS", wo_mm_all)
    wait("tensor", "vS", max(b_done, tern_all))
    ntok_free = [0, 0]
    nfm_free = [0, 0, 0]
    ffn_bank_free = [0] * 8
    tr_evac = {}
    last_store = 0
    for t in range(TB):
        tok_sl = t % 2
        fm_sl = t % 3
        tok_ap = S3[:, tok_sl * 2048:(tok_sl + 1) * 2048]
        fm_base = 4096 + fm_sl * 2048
        wait("vector", "tS", ntok_free[tok_sl])
        vop(lambda e, o=WFT[:, 0:2048], i_=RREG[:, t * C:(t + 1) * C],
            b=STA[:, t * 8 + 6:t * 8 + 7], g=GAM[:]:
            e.scalar_tensor_tensor(o, i_, b, g, ALU.mult, ALU.mult), inc=False)
        vop(lambda e, o=WFT[:, 0:2048], i_=WFT[:, 0:2048]:
            e.tensor_scalar(o, i_, -128.0, 127.0, ALU.max, ALU.min), inc=False)
        v_q = vop(lambda e, o=tok_ap, i_=WFT[:, 0:2048]:
                  e.tensor_scalar(o, i_, MAGIC, -MAGIC, ALU.add, ALU.add))
        wait("tensor", "vS", v_q)
        for c in range(CB):
            bank = c % 2
            off = ((c // 2) % 4) * 128
            if (t, c - 2) in tr_evac:
                wait("tensor", "sS", tr_evac[(t, c - 2)])
            elif c < 2 and t > 0 and (t - 1, 14 + c) in tr_evac:
                wait("tensor", "sS", tr_evac[(t - 1, 14 + c)])
            tr = top(lambda e, o=PT[bank][:, off:off + 128],
                     i_=tok_ap[:, c * 128:(c + 1) * 128], iden=IDB[:, 0:128]:
                     e.transpose(o, i_, iden))
            wait("scalar", "tS", tr)
            tr_evac[(t, c)] = sop(lambda e, o=S3[:, fm_base + c * 128: fm_base + (c + 1) * 128],
                                  p=PT[bank][:, off:off + 128]: e.copy(out=o, in_=p))
        ntok_free[tok_sl] = P.cur("tS")
        trans_s = tr_evac[(t, CB - 1)]
        wait("tensor", "sS", trans_s)
        pb = 0
        for osl in range(4):
            wait("tensor", "vS", ffn_bank_free[pb + osl])
        for c in range(CB):
            lhs = S3[:, fm_base + c * 128: fm_base + (c + 1) * 128]
            for osl in range(4):
                zone = S2 if osl < 2 else S1
                rhs = zone[:, c * 1024 + (osl % 2) * 512: c * 1024 + (osl % 2) * 512 + 512]
                top(lambda e, o=PS[pb + osl][:], l=lhs, r=rhs, st=(c == 0), sp=(c == CB - 1):
                    e.matmul(o, l, r, start=st, stop=sp),
                    inc=(c == CB - 1 and osl == 3))
        ffn_done = P.cur("tS")
        nfm_free[fm_sl] = ffn_done
        wait("vector", "tS", ffn_done)
        for osl in range(4):
            ev = vop(lambda e, o=RREG[:, t * C + osl * 512: t * C + (osl + 1) * 512],
                     p=PS[pb + osl][:], sc=COLS[:, 5:6]:
                     e.scalar_tensor_tensor(o, p, sc, o, ALU.mult, ALU.add))
            ffn_bank_free[pb + osl] = ev
        for g in range(4):
            vop(lambda e, o=BNS[:, g * 6:(g + 1) * 6],
                i_=RREG[:, t * C + g * 512: t * C + (g + 1) * 512]:
                e.bn_stats(o, i_), inc=(g == 3))
        bn2 = P.cur("vS")
        wait("vector", "vS", bn2)
        t_ = vop(lambda e, o=STA[:, t * 8 + 6:t * 8 + 8], i_=BNS[:, 0:24]: e.bn_aggr(o, i_))
        wait("vector", "vS", t_)
        v_u2 = vop(lambda e, o=STA[:, t * 8 + 7:t * 8 + 8], i_=STA[:, t * 8 + 7:t * 8 + 8]:
                   e.tensor_scalar(o, i_, EPS_LN, None, ALU.add))
        wait("scalar", "vS", v_u2)
        s_sq2 = sop(lambda e, o=STA[:, t * 8 + 7:t * 8 + 8], i_=STA[:, t * 8 + 7:t * 8 + 8]:
                    e.activation(out=o, in_=i_, func=AF.Sqrt))
        wait("vector", "sS", s_sq2)
        t_ = vop(lambda e, o=STA[:, t * 8 + 7:t * 8 + 8], i_=STA[:, t * 8 + 7:t * 8 + 8]:
                 e.reciprocal(o, i_))
        wait("vector", "vS", t_)
        v_out = vop(lambda e, o=RREG[:, t * C:(t + 1) * C], i_=RREG[:, t * C:(t + 1) * C],
                    m=STA[:, t * 8 + 6:t * 8 + 7], s2=STA[:, t * 8 + 7:t * 8 + 8]:
                    e.tensor_scalar(o, i_, m, s2, ALU.subtract, ALU.mult))
        wait("sync", "vS", v_out)
        last_store = dma("sync", "dS", y_d[t * 128:(t + 1) * 128, :], RREG[:, t * C:(t + 1) * C])

    wait("sync", "dS", last_store)
    wait("gpsimd", "dS", last_store)

    with es:
        block = es.enter_context(nc.Block())

        @block.sync
        def _(e):
            for f in P.ins["sync"]:
                f(e)

        @block.gpsimd
        def _(e):
            for f in P.ins["gpsimd"]:
                f(e)

        @block.tensor
        def _(e):
            for f in P.ins["tensor"]:
                f(e)

        @block.vector
        def _(e):
            for f in P.ins["vector"]:
                f(e)

        @block.scalar
        def _(e):
            for f in P.ins["scalar"]:
                f(e)

    return nc


def kernel(L0_attn_x, L0_attn_cos, L0_attn_sin, L0_attn_Wq, L0_attn_Wk, L0_attn_Wv,
           L0_attn_Wo, L0_ffn_weight, L0_ffn_norm_weight):
    x = np.ascontiguousarray(np.asarray(L0_attn_x, dtype=np.float32)[0])
    xT = np.ascontiguousarray(x.T)
    wq_t = np.ascontiguousarray(np.asarray(L0_attn_Wq, dtype=np.float32).T)
    wk_t = np.ascontiguousarray(np.asarray(L0_attn_Wk, dtype=np.float32).T)
    wv_t = np.ascontiguousarray(np.asarray(L0_attn_Wv, dtype=np.float32).T)
    wo_t = np.ascontiguousarray(np.asarray(L0_attn_Wo, dtype=np.float32).T)
    wf_t = np.ascontiguousarray(np.asarray(L0_ffn_weight, dtype=np.float32).T)
    gam = np.ascontiguousarray(np.broadcast_to(
        np.asarray(L0_ffn_norm_weight, dtype=np.float32)[None, :], (128, C)))
    iden = np.eye(128, dtype=np.float32)
    ones = np.ones((128, 128), dtype=np.float32)

    nc = build_nc()
    in_maps = []
    for cid in range(NCORES):
        in_maps.append({
            "x_fm": np.ascontiguousarray(xT[:, cid * T:(cid + 1) * T]),
            "x_tok": np.ascontiguousarray(x[cid * T:(cid + 1) * T, :]),
            "wq_t": wq_t, "wk_t": wk_t, "wv_t": wv_t, "wo_t": wo_t, "wf_t": wf_t,
            "gamma_bc": gam, "id_bf": iden, "ones_f": ones,
        })
    res = run_bass_kernel_spmd(nc, in_maps, core_ids=list(range(NCORES)))
    out = np.concatenate([res.results[i]["y"] for i in range(NCORES)], axis=0)
    return out.reshape(1, 8192, C)


# version 09
# speedup vs baseline: 1.1934x; 1.0050x over previous
import contextlib
import numpy as np
import concourse.bass as bass
import concourse.mybir as mybir
from concourse.bass_utils import run_bass_kernel_spmd

F32 = mybir.dt.float32
BF16 = mybir.dt.bfloat16
AF = mybir.ActivationFunctionType
ALU = mybir.AluOpType
AX = mybir.AxisListType.X

NCORES = 8
T = 1024
TB = 8
C = 2048
CB = 16
H = 16
MAGIC = 12582912.0
EPS_LN = 1e-5
EPS_RMS = 1e-6
EPS_SCALE = 1e-5
EPS_Z = 1e-6


class Prog:
    ENGINES = ("sync", "gpsimd", "tensor", "vector", "scalar")

    def __init__(self):
        self.ins = {e: [] for e in self.ENGINES}
        self.sem_ct = {}

    def emit(self, eng, fn):
        self.ins[eng].append(fn)

    def inc(self, sem, amount):
        self.sem_ct[sem] = self.sem_ct.get(sem, 0) + amount
        return self.sem_ct[sem]

    def cur(self, sem):
        return self.sem_ct.get(sem, 0)


def build_nc():
    nc = bass.Bass("TRN2", target_bir_lowering=False, debug=False)

    x_fm_d = nc.dram_tensor("x_fm", [C, T], F32, kind="ExternalInput").ap()
    x_tok_d = nc.dram_tensor("x_tok", [T, C], F32, kind="ExternalInput").ap()
    wq_d = nc.dram_tensor("wq_t", [C, C], F32, kind="ExternalInput").ap()
    wk_d = nc.dram_tensor("wk_t", [C, C], F32, kind="ExternalInput").ap()
    wv_d = nc.dram_tensor("wv_t", [C, C], F32, kind="ExternalInput").ap()
    wo_d = nc.dram_tensor("wo_t", [C, C], F32, kind="ExternalInput").ap()
    wf_d = nc.dram_tensor("wf_t", [C, C], F32, kind="ExternalInput").ap()
    gam_d = nc.dram_tensor("gamma_bc", [128, C], F32, kind="ExternalInput").ap()
    id_d = nc.dram_tensor("id_bf", [128, 128], F32, kind="ExternalInput").ap()
    ones_d = nc.dram_tensor("ones_f", [128, 128], F32, kind="ExternalInput").ap()
    y_d = nc.dram_tensor("y", [T, C], F32, kind="ExternalOutput").ap()

    cc1_in = nc.dram_tensor("cc1_in", [128, 2064], F32)
    cc1_out = nc.dram_tensor("cc1_out", [128, 2064], F32)
    cc2_in = nc.dram_tensor("cc2_in", [1, 128], F32)
    cc2_out = nc.dram_tensor("cc2_out", [1, 128], F32)

    P = Prog()
    es = contextlib.ExitStack()

    def sb(name, shape, dtype=F32):
        return es.enter_context(nc.sbuf_tensor(name, shape, dtype))

    RREG = sb("RREG", [128, TB * C])
    GAM = sb("GAM", [128, C])
    WFT = sb("WFT", [128, C])
    FSCR = sb("FSCR", [128, 1024])
    WST = sb("WST", [128, 1536])
    S1f = sb("S1f", [128, 8192])
    S2f = sb("S2f", [128, 8192])
    S3f = sb("S3f", [128, 8192])
    S1 = S1f.bitcast(BF16)
    S2 = S2f.bitcast(BF16)
    S3 = S3f.bitcast(BF16)
    WSTB = sb("WSTB", [128, 768]).bitcast(BF16)
    KVSB = sb("KVSB", [128, 1032]).bitcast(BF16)
    ZB = sb("ZB", [128, 256]).bitcast(BF16)
    IDB = sb("IDB", [128, 64]).bitcast(BF16)
    ONEB = sb("ONEB", [128, 64]).bitcast(BF16)
    ZBC = sb("ZBC", [128, 512])
    ZTOK = sb("ZTOK", [128, 128])
    BNS = sb("BNS", [128, 24])
    STA = sb("STA", [128, 64])
    ACC = sb("ACC", [128, 32])
    COLS = sb("COLS", [128, 8])
    ROWS = sb("ROWS", [1, 396])
    ZR2 = sb("ZR2", [1, 1024]).bitcast(BF16)
    ZR = sb("ZR", [1, 1024]).bitcast(BF16)
    ONESF = sb("ONESF", [1, 128])

    KVP = RREG[:, 1024:3088]

    PS = [es.enter_context(nc.psum_tensor(f"ps{i}", [128, 512], F32)) for i in range(6)]
    PT = [es.enter_context(nc.psum_tensor(f"pt{i}", [128, 1024], BF16)) for i in range(2)]
    PS.append(PT[0].bitcast(F32))
    PS.append(PT[1].bitcast(F32))

    sems = {}
    for s in ("dS", "dG", "dC", "tS", "vS", "sS", "cc",
              "dW0", "dW1", "dW2", "dF0", "dF1", "dX", "dZ", "dZ2", "dC2"):
        sems[s] = es.enter_context(nc.semaphore(s))

    def dma(eng, qsem, out, in_):
        tgt = P.inc(qsem, 16)
        P.emit(eng, lambda e, o=out, i_=in_, s=sems[qsem]: e.dma_start(out=o, in_=i_).then_inc(s, 16))
        return tgt

    def wait(eng, sem, tgt):
        if tgt and tgt > 0:
            P.emit(eng, lambda e, s=sems[sem], t=tgt: e.wait_ge(s, t))

    def vop(fn, inc=True):
        if inc:
            tgt = P.inc("vS", 1)
            P.emit("vector", lambda e, f=fn, s=sems["vS"]: f(e).then_inc(s, 1))
            return tgt
        P.emit("vector", lambda e, f=fn: f(e))
        return P.cur("vS")

    def sop(fn, inc=True):
        if inc:
            tgt = P.inc("sS", 1)
            P.emit("scalar", lambda e, f=fn, s=sems["sS"]: f(e).then_inc(s, 1))
            return tgt
        P.emit("scalar", lambda e, f=fn: f(e))
        return P.cur("sS")

    def top(fn, inc=True):
        if inc:
            tgt = P.inc("tS", 1)
            P.emit("tensor", lambda e, f=fn, s=sems["tS"]: f(e).then_inc(s, 1))
            return tgt
        P.emit("tensor", lambda e, f=fn: f(e))
        return P.cur("tS")

    d_id = dma("gpsimd", "dG", IDB[:, 0:128], id_d)
    d_ones = dma("gpsimd", "dG", ONEB[:, 0:128], ones_d)
    d_onesf = dma("sync", "dS", ONESF[0:1, :], ones_d[0:1, :])
    d_gam = dma("sync", "dS", GAM[:], gam_d)
    d_xfm_last = 0
    for c in range(CB):
        d_xfm_last = dma("gpsimd", "dG", S1[:, c * T:(c + 1) * T], x_fm_d[c * 128:(c + 1) * 128, :])

    class WStream:
        def __init__(self):
            self.i = 0
            self.f32_free = [0, 0, 0]
            self.bf_free = [0, 0, 0]

        def load(self, src_ap):
            sl = self.i % 3
            self.i += 1
            f32_ap = WST[:, sl * 512:(sl + 1) * 512]
            bf_ap = WSTB[:, sl * 512:(sl + 1) * 512]
            wait("sync", "sS", self.f32_free[sl])
            d_tgt = dma("sync", f"dW{sl}", f32_ap, src_ap)
            wait("scalar", f"dW{sl}", d_tgt)
            wait("scalar", "tS", self.bf_free[sl])
            c_tgt = sop(lambda e, o=bf_ap, i_=f32_ap: e.copy(out=o, in_=i_))
            self.f32_free[sl] = c_tgt
            return bf_ap, c_tgt, sl

        def consumed(self, sl, t_tgt):
            self.bf_free[sl] = t_tgt

    ws = WStream()

    wait("tensor", "dG", d_xfm_last)
    bank_free = [0] * 8

    def proj_slice(osl, which):
        wsrc = wk_d if which == "k" else wv_d
        base = 0 if which == "k" else 4096
        tiles = {}

        def _load(c):
            tiles[c] = ws.load(wsrc[c * 128:(c + 1) * 128, osl * 512:(osl + 1) * 512])
        for c in range(3):
            _load(c)
        for c in range(CB):
            bf_ap, c_tgt, sl = tiles[c]
            wait("tensor", "sS", c_tgt)
            for t in range(TB):
                if c == 0:
                    wait("tensor", "vS", bank_free[t])
                lhs = S1[:, c * T + t * 128: c * T + (t + 1) * 128]
                top(lambda e, o=PS[t][:], l=lhs, r=bf_ap, st=(c == 0), sp=(c == CB - 1):
                    e.matmul(o, l, r, start=st, stop=sp), inc=(t == TB - 1))
            ws.consumed(sl, P.cur("tS"))
            if c + 3 < CB:
                _load(c + 3)
        mm_done = P.cur("tS")
        evac = [0] * TB
        if which == "k":
            exps = [0] * TB

            def _stt(t):
                wait("vector", "sS", exps[t])
                dstt = S2[:, base + t * 512: base + (t + 1) * 512]
                scr2 = FSCR[:, (t % 2) * 512:(t % 2) * 512 + 512]
                evac[t] = vop(lambda e, o=dstt, p=PS[t][:], x2=scr2:
                              e.scalar_tensor_tensor(o, p, 0.0, x2, ALU.max, ALU.add))
                bank_free[t] = evac[t]
            wait("vector", "tS", mm_done)
            for t in range(TB):
                scr = FSCR[:, (t % 2) * 512:(t % 2) * 512 + 512]
                v_min = vop(lambda e, o=scr, p=PS[t][:]: e.tensor_scalar(o, p, 0.0, None, ALU.min))
                if t >= 1:
                    _stt(t - 1)
                wait("scalar", "vS", v_min)
                exps[t] = sop(lambda e, o=scr, i_=scr: e.activation(out=o, in_=i_, func=AF.Exp))
            _stt(TB - 1)
        else:
            wait("vector", "tS", mm_done)
            for t in range(TB):
                dst = S2[:, base + t * 512: base + (t + 1) * 512]
                ev = vop(lambda e, o=dst, p=PS[t][:]: e.tensor_copy(o, p))
                evac[t] = ev
                bank_free[t] = ev
        return evac

    class Absum:
        def __init__(self):
            self.di = 0
            self.ci = 0
            self.dtgt = {}
            self.rtgt = {}

        def _dma_one(self):
            i = self.di
            self.di += 1
            ch, hf = i // 2, i % 2
            if i >= 2:
                wait("gpsimd", "vS", self.rtgt[i - 2])
            self.dtgt[i] = dma("gpsimd", f"dF{i % 2}", WFT[:, (i % 2) * 1024:(i % 2) * 1024 + 1024],
                               wf_d[ch * 128:(ch + 1) * 128, hf * 1024:(hf + 1) * 1024])

        def emit(self, n):
            for _ in range(n):
                if self.ci >= 32:
                    return
                while self.di < min(self.ci + 2, 32):
                    self._dma_one()
                i = self.ci
                self.ci += 1
                wait("vector", f"dF{i % 2}", self.dtgt[i])
                self.rtgt[i] = vop(lambda e, o=ACC[:, i:i + 1],
                                   i_=WFT[:, (i % 2) * 1024:(i % 2) * 1024 + 1024]:
                                   e.tensor_reduce(o, i_, axis=AX, op=ALU.add,
                                                   apply_absolute_value=True))
                while self.di < min(self.ci + 2, 32):
                    self._dma_one()

    absum = Absum()

    kv_evac_last = 0
    for osl in range(4):
        ek = proj_slice(osl, "k")
        absum.emit(4)
        ev = proj_slice(osl, "v")
        absum.emit(4)
        wait("tensor", "vS", bank_free[0])
        wait("tensor", "vS", bank_free[1])
        wait("tensor", "vS", max(ek[-1], ev[-1]))
        for t in range(TB):
            for hh in range(4):
                lhs = S2[:, t * 512 + hh * 128: t * 512 + (hh + 1) * 128]
                rhs = S2[:, 4096 + t * 512 + hh * 128: 4096 + t * 512 + (hh + 1) * 128]
                top(lambda e, o=PS[0][:, hh * 128:(hh + 1) * 128], l=lhs, r=rhs,
                    st=(t == 0 and hh == 0), sp=(t == TB - 1 and hh == 3):
                    e.matmul(o, l, r, start=st, stop=sp, skip_group_check=True), inc=False)
                top(lambda e, o=PS[1][:, hh:hh + 1], l=lhs, r=ONEB[:, 0:1],
                    st=(t == 0 and hh == 0), sp=(t == TB - 1 and hh == 3):
                    e.matmul(o, l, r, start=st, stop=sp, skip_group_check=True),
                    inc=(t == TB - 1 and hh == 3))
        kv_done_t = P.cur("tS")
        wait("vector", "tS", kv_done_t)
        vop(lambda e, o=KVP[:, osl * 512:(osl + 1) * 512], p=PS[0][:]: e.tensor_copy(o, p), inc=False)
        kv_evac_last = vop(lambda e, o=KVP[:, 2048 + osl * 4: 2048 + (osl + 1) * 4],
                           p=PS[1][:, 0:4]: e.tensor_copy(o, p))
        bank_free[0] = kv_evac_last
        bank_free[1] = kv_evac_last
    kv_mm_all = P.cur("tS")
    absum.emit(32)
    absum_last = absum.rtgt[31]

    wait("vector", "vS", absum_last)
    v_asum = vop(lambda e, o=COLS[:, 2:3], i_=ACC[:, 0:32]: e.tensor_reduce(o, i_, axis=AX, op=ALU.add))
    wait("vector", "vS", v_asum)
    wait("sync", "vS", v_asum)
    d_g1 = dma("sync", "dZ", ROWS[0:1, 0:128], COLS[:, 2:3])
    wait("vector", "dZ", d_g1)
    v_w1 = vop(lambda e, o=ROWS[0:1, 128:129], i_=ROWS[0:1, 0:128]:
               e.tensor_reduce(o, i_, axis=AX, op=ALU.add))
    wait("vector", "vS", v_w1)
    t1_ = vop(lambda e, o=ROWS[0:1, 129:130], i_=ROWS[0:1, 128:129]:
        e.tensor_scalar(o, i_, 1.0 / (C * C), EPS_SCALE, ALU.mult, ALU.max))
    wait("vector", "vS", t1_)
    t2_ = vop(lambda e, o=ROWS[0:1, 130:131], i_=ROWS[0:1, 129:130]: e.reciprocal(o, i_))
    wait("vector", "vS", t2_)
    wait("vector", "dS", d_onesf)
    v_w4 = vop(lambda e, o=ROWS[0:1, 140:268], i_=ONESF[0:1, :]:
               e.tensor_scalar(o, i_, ROWS[0:1, 130:131], None, ALU.mult))
    wait("sync", "vS", v_w4)
    d_invws = dma("sync", "dZ", COLS[:, 0:1], ROWS[0:1, 140:268])

    wait("gpsimd", "vS", kv_evac_last)
    d_c1 = dma("gpsimd", "dG", cc1_in.ap(), KVP[:])
    wait("gpsimd", "dG", d_c1)
    cc1_tgt = P.inc("cc", 1)
    P.emit("gpsimd", lambda e: e.collective_compute(
        "AllReduce", ALU.add, replica_groups=[list(range(NCORES))],
        ins=[cc1_in.ap()], outs=[cc1_out.ap()]).then_inc(sems["cc"], 1))
    wait("gpsimd", "cc", cc1_tgt)
    d_c1b = dma("gpsimd", "dG", KVP[:], cc1_out.ap())

    tern_done = []

    class Ternarizer:
        def __init__(self, half, zone, tens_gate):
            self.half, self.zone, self.gate = half, zone, tens_gate
            self.di = 0
            self.ci = 0
            self.dtgt = {}
            self.ctgt = {}

        def _dma_one(self):
            ch = self.di
            self.di += 1
            if ch < 2 and self.half == 0:
                wait("gpsimd", "vS", absum_last)
            if ch >= 2:
                wait("gpsimd", "vS", self.ctgt[ch - 2])
            elif self.half == 1 and ch < 2:
                wait("gpsimd", "vS", tern_done[CB - 2 + ch])
            self.dtgt[ch] = dma("gpsimd", f"dF{ch % 2}", WFT[:, (ch % 2) * 1024:(ch % 2) * 1024 + 1024],
                                wf_d[ch * 128:(ch + 1) * 128,
                                     self.half * 1024:(self.half + 1) * 1024])

        def emit(self, n):
            for _ in range(n):
                if self.ci >= CB:
                    return
                while self.di < min(self.ci + 2, CB):
                    self._dma_one()
                ch = self.ci
                self.ci += 1
                f32_ap = WFT[:, (ch % 2) * 1024:(ch % 2) * 1024 + 1024]
                wait("vector", f"dF{ch % 2}", self.dtgt[ch])
                if ch == 0:
                    wait("vector", "dZ", d_invws)
                    wait("vector", "tS", self.gate)
                dst = self.zone[:, ch * 1024:(ch + 1) * 1024]
                vop(lambda e, o=f32_ap, i_=f32_ap:
                    e.tensor_scalar(o, i_, COLS[:, 0:1], MAGIC, ALU.mult, ALU.add), inc=False)
                vop(lambda e, o=f32_ap, i_=f32_ap:
                    e.tensor_scalar(o, i_, -MAGIC, -1.0, ALU.add, ALU.max), inc=False)
                t3 = vop(lambda e, o=dst, i_=f32_ap: e.tensor_scalar(o, i_, 1.0, None, ALU.min))
                self.ctgt[ch] = t3
                tern_done.append(t3)
                while self.di < min(self.ci + 2, CB):
                    self._dma_one()

    tern1 = Ternarizer(0, S2, kv_mm_all)
    qb_free = list(bank_free)
    for grp in range(4):
        tiles = {}

        def _loadq(c):
            tiles[c] = ws.load(wq_d[c * 128:(c + 1) * 128, grp * 512:(grp + 1) * 512])
        for c in range(3):
            _loadq(c)
        for c in range(CB):
            bf_ap, c_tgt, sl = tiles[c]
            wait("tensor", "sS", c_tgt)
            for j in range(4):
                for th in range(2):
                    bank = 2 * j + th
                    if c == 0:
                        wait("tensor", "vS", qb_free[bank])
                    lhs = bf_ap[:, j * 128:(j + 1) * 128]
                    rhs = S1[:, c * T + th * 512: c * T + (th + 1) * 512]
                    top(lambda e, o=PS[bank][:], l=lhs, r=rhs, st=(c == 0), sp=(c == CB - 1):
                        e.matmul(o, l, r, start=st, stop=sp),
                        inc=(j == 3 and th == 1))
            ws.consumed(sl, P.cur("tS"))
            if c + 3 < CB:
                _loadq(c + 3)
        q_done = P.cur("tS")
        exps = [0] * 8

        def _qstt(bank):
            j2, th2 = bank // 2, bank % 2
            h2 = grp * 4 + j2
            dstt = S3[:, h2 * T + th2 * 512: h2 * T + (th2 + 1) * 512]
            scr2 = FSCR[:, (bank % 2) * 512:(bank % 2) * 512 + 512]
            wait("vector", "sS", exps[bank])
            qb_free[bank] = vop(lambda e, o=dstt, p=PS[bank][:], x2=scr2:
                                e.scalar_tensor_tensor(o, p, 0.0, x2, ALU.max, ALU.add))
        wait("vector", "tS", q_done)
        for bank in range(8):
            scr = FSCR[:, (bank % 2) * 512:(bank % 2) * 512 + 512]
            v_min = vop(lambda e, o=scr, p=PS[bank][:]: e.tensor_scalar(o, p, 0.0, None, ALU.min))
            if bank >= 1:
                _qstt(bank - 1)
            wait("scalar", "vS", v_min)
            exps[bank] = sop(lambda e, o=scr, i_=scr: e.activation(out=o, in_=i_, func=AF.Exp))
        _qstt(7)
        tern1.emit(4)
    tern1.emit(CB)
    q_mm_all = P.cur("tS")
    q_evac_all = P.cur("vS")

    wait("vector", "dG", d_c1b)
    v_kvbf = vop(lambda e, o=KVSB[:, 0:2064], i_=KVP[:]: e.tensor_copy(o, i_))

    wait("tensor", "vS", v_kvbf)
    wait("tensor", "vS", max(qb_free))
    for h in range(H):
        for t in range(TB):
            lhs = S3[:, h * T + t * 128: h * T + (t + 1) * 128]
            col = t * 16 + h
            top(lambda e, o=PS[2][:, col:col + 1], l=lhs, r=KVSB[:, 2048 + h:2049 + h]:
                e.matmul(o, l, r, start=True, stop=True, skip_group_check=True),
                inc=(h == H - 1 and t == TB - 1))
    den_done = P.cur("tS")
    wait("vector", "tS", den_done)
    t1_ = vop(lambda e, o=ZTOK[:], p=PS[2][:, 0:128]: e.tensor_scalar(o, p, EPS_Z, None, ALU.add))
    wait("vector", "vS", t1_)
    t2_ = vop(lambda e, o=ZTOK[:], i_=ZTOK[:]: e.reciprocal(o, i_))
    wait("vector", "vS", t2_)
    t3_ = vop(lambda e, o=ZB[:, 0:128], i_=ZTOK[:]: e.tensor_copy(o, i_))
    wait("vector", "vS", t3_)
    v_zlo = vop(lambda e, o=ZB[:, 128:256], a=ZTOK[:], b=ZB[:, 0:128]:
                e.tensor_tensor(o, a, b, ALU.subtract))
    wait("tensor", "vS", v_zlo)
    top(lambda e, o=PT[0][:, 0:128], i_=ZB[:, 0:128], iden=IDB[:, 0:128]:
        e.transpose(o, i_, iden), inc=False)
    tr_z = top(lambda e, o=PT[0][:, 128:256], i_=ZB[:, 128:256], iden=IDB[:, 0:128]:
               e.transpose(o, i_, iden))
    wait("scalar", "tS", tr_z)
    sop(lambda e, o=ZB[:, 256:384], p=PT[0][:, 0:128]: e.copy(out=o, in_=p), inc=False)
    s_zrows = sop(lambda e, o=ZB[:, 384:512], p=PT[0][:, 128:256]: e.copy(out=o, in_=p))

    zhi_rows = ZB[:, 256:384]
    zlo_rows = ZB[:, 384:512]
    wait("tensor", "sS", s_zrows)
    wait("sync", "sS", s_zrows)
    wait("scalar", "sS", s_zrows)
    pair_gate = [0, 0]
    zr_gate = [0, 0]
    d_zr = {}

    def _gather(h):
        pair = h % 2
        buf = ZR if pair == 0 else ZR2
        hisem = "dZ" if pair == 0 else "dZ2"
        losem = "dC" if pair == 0 else "dC2"
        wait("sync", "tS", zr_gate[pair])
        wait("scalar", "tS", zr_gate[pair])
        d_hi = dma("sync", hisem, buf[0:1, 0:1024], zhi_rows[h:128:16, 0:128])
        d_lo = dma("scalar", losem, buf[0:1, 1024:2048], zlo_rows[h:128:16, 0:128])
        d_zr[h] = (hisem, d_hi, losem, d_lo)
    _gather(0)
    _gather(1)
    for h in range(H):
        pair = h % 2
        zb0, zb1 = (4, 5) if pair == 0 else (2, 3)
        ob0 = 6 if pair == 0 else 0
        buf = ZR if h % 2 == 0 else ZR2
        wait("tensor", "vS", pair_gate[pair])
        wait("tensor", d_zr[h][0], d_zr[h][1])
        wait("tensor", d_zr[h][2], d_zr[h][3])
        for half, bank in ((0, zb0), (1, zb1)):
            top(lambda e, o=PS[bank][:], l=ONEB[0:1, 0:128],
                r=buf[0:1, half * 512: (half + 1) * 512]:
                e.matmul(o, l, r, start=True, stop=False, skip_group_check=True), inc=False)
            top(lambda e, o=PS[bank][:], l=ONEB[0:1, 0:128],
                r=buf[0:1, 1024 + half * 512: 1024 + (half + 1) * 512]:
                e.matmul(o, l, r, start=False, stop=True, skip_group_check=True),
                inc=(half == 1))
        zbc_done = P.cur("tS")
        zr_gate[pair] = zbc_done
        if h + 2 < H:
            _gather(h + 2)
        for th in range(2):
            top(lambda e, o=PS[ob0 + th][:], l=KVSB[:, h * 128:(h + 1) * 128],
                r=S3[:, h * T + th * 512: h * T + (th + 1) * 512]:
                e.matmul(o, l, r, start=True, stop=True), inc=(th == 1))
        out_done = P.cur("tS")
        wait("vector", "tS", out_done)
        ev = 0
        for th, zb in ((0, zb0), (1, zb1)):
            wait("scalar", "tS", zbc_done)
            s_zb = sop(lambda e, o=ZBC[:], p=PS[zb][:]: e.copy(out=o, in_=p))
            wait("vector", "sS", s_zb)
            ev = vop(lambda e, o=S3[:, h * T + th * 512: h * T + (th + 1) * 512],
                     p=PS[ob0 + th][:], zz=ZBC[:]:
                     e.tensor_tensor(o, p, zz, ALU.mult))
        pair_gate[pair] = ev
    out_phase_v = P.cur("vS")

    wait("sync", "vS", v_kvbf)
    d_xtok = [dma("sync", "dX", RREG[:, t * C:(t + 1) * C], x_tok_d[t * 128:(t + 1) * 128, :])
              for t in range(TB)]

    tern2 = Ternarizer(1, S1, q_mm_all)

    wait("tensor", "vS", out_phase_v)
    wait("tensor", "dX", d_xtok[-1])
    bank_free2 = [0] * 8
    for osl in range(4):
        tiles = {}

        def _loadw(c):
            tiles[c] = ws.load(wo_d[c * 128:(c + 1) * 128, osl * 512:(osl + 1) * 512])
        for c in range(3):
            _loadw(c)
        for c in range(CB):
            bf_ap, c_tgt, sl = tiles[c]
            wait("tensor", "sS", c_tgt)
            for t in range(TB):
                if c == 0:
                    wait("tensor", "vS", bank_free2[t])
                lhs = S3[:, c * T + t * 128: c * T + (t + 1) * 128]
                top(lambda e, o=PS[t][:], l=lhs, r=bf_ap, st=(c == 0), sp=(c == CB - 1):
                    e.matmul(o, l, r, start=st, stop=sp), inc=(t == TB - 1))
            ws.consumed(sl, P.cur("tS"))
            if c + 3 < CB:
                _loadw(c + 3)
        wo_done = P.cur("tS")
        for t in range(TB):
            wait("vector", "tS", wo_done)
            dst = RREG[:, t * C + osl * 512: t * C + (osl + 1) * 512]
            ev = vop(lambda e, o=dst, p=PS[t][:]:
                     e.scalar_tensor_tensor(o, p, 1.0, o, ALU.mult, ALU.add))
            bank_free2[t] = ev
        tern2.emit(4)
    tern2.emit(CB)
    wo_mm_all = P.cur("tS")
    tern_all = tern_done[-1]

    for t in range(TB):
        for g in range(4):
            vop(lambda e, o=BNS[:, g * 6:(g + 1) * 6],
                i_=RREG[:, t * C + g * 512: t * C + (g + 1) * 512]:
                e.bn_stats(o, i_), inc=(g == 3))
        bn_tgt = P.cur("vS")
        wait("vector", "vS", bn_tgt)
        for fn_ in (
            lambda e, t=t: e.bn_aggr(STA[:, t * 8:t * 8 + 2], BNS[:, 0:24]),
            lambda e, t=t: e.tensor_scalar(STA[:, t * 8 + 2:t * 8 + 3],
                                           STA[:, t * 8 + 1:t * 8 + 2], EPS_LN, None, ALU.add),
            lambda e, t=t: e.reciprocal(STA[:, t * 8 + 3:t * 8 + 4], STA[:, t * 8 + 2:t * 8 + 3]),
            lambda e, t=t: e.tensor_tensor(STA[:, t * 8 + 4:t * 8 + 5], STA[:, t * 8 + 1:t * 8 + 2],
                                           STA[:, t * 8 + 3:t * 8 + 4], ALU.mult),
            lambda e, t=t: e.tensor_scalar(STA[:, t * 8 + 4:t * 8 + 5], STA[:, t * 8 + 4:t * 8 + 5],
                                           EPS_RMS, None, ALU.add),
        ):
            v_m2 = vop(fn_)
            wait("vector", "vS", v_m2)
        wait("scalar", "vS", v_m2)
        sop(lambda e, o=STA[:, t * 8 + 2:t * 8 + 3], i_=STA[:, t * 8 + 2:t * 8 + 3]:
            e.activation(out=o, in_=i_, func=AF.Sqrt), inc=False)
        s_sq = sop(lambda e, o=STA[:, t * 8 + 4:t * 8 + 5], i_=STA[:, t * 8 + 4:t * 8 + 5]:
                   e.activation(out=o, in_=i_, func=AF.Sqrt))
        wait("vector", "sS", s_sq)
        t_ = vop(lambda e, o=STA[:, t * 8 + 2:t * 8 + 3], i_=STA[:, t * 8 + 2:t * 8 + 3]:
                 e.reciprocal(o, i_))
        wait("vector", "vS", t_)
        t_ = vop(lambda e, o=STA[:, t * 8 + 4:t * 8 + 5], i_=STA[:, t * 8 + 4:t * 8 + 5]:
                 e.reciprocal(o, i_))
        wait("vector", "vS", t_)
        vop(lambda e, o=RREG[:, t * C:(t + 1) * C], i_=RREG[:, t * C:(t + 1) * C],
            m=STA[:, t * 8:t * 8 + 1], s2=STA[:, t * 8 + 2:t * 8 + 3]:
            e.tensor_scalar(o, i_, m, s2, ALU.subtract, ALU.mult), inc=False)
        if t == 0:
            wait("vector", "dS", d_gam)
        v_xg = vop(lambda e, o=WFT[:, 0:2048], a=RREG[:, t * C:(t + 1) * C], b=GAM[:]:
                   e.tensor_tensor(o, a, b, ALU.mult))
        wait("vector", "vS", v_xg)
        t_ = vop(lambda e, o=STA[:, t * 8 + 5:t * 8 + 6], i_=WFT[:, 0:2048]:
                 e.tensor_reduce(o, i_, axis=AX, op=ALU.max, apply_absolute_value=True))
        wait("vector", "vS", t_)
        vop(lambda e, o=STA[:, t * 8 + 5:t * 8 + 6], a=STA[:, t * 8 + 5:t * 8 + 6],
            b=STA[:, t * 8 + 4:t * 8 + 5]: e.tensor_tensor(o, a, b, ALU.mult),
            inc=(t == TB - 1))
    v_cmb = vop(lambda e, o=COLS[:, 3:4], i_=STA[:, 5:6]: e.tensor_copy(o, i_))
    for t in range(1, TB):
        wait("vector", "vS", v_cmb)
        v_cmb = vop(lambda e, o=COLS[:, 3:4], a=COLS[:, 3:4], b=STA[:, t * 8 + 5:t * 8 + 6]:
                    e.tensor_tensor(o, a, b, ALU.max))
    wait("sync", "vS", v_cmb)
    d_g2 = dma("sync", "dZ", ROWS[0:1, 0:128], COLS[:, 3:4])

    wait("gpsimd", "dZ", d_g2)
    d_c2 = dma("gpsimd", "dG", cc2_in.ap(), ROWS[0:1, 0:128])
    wait("gpsimd", "dG", d_c2)
    cc2_tgt = P.inc("cc", 1)
    P.emit("gpsimd", lambda e: e.collective_compute(
        "AllReduce", ALU.max, replica_groups=[list(range(NCORES))],
        ins=[cc2_in.ap()], outs=[cc2_out.ap()]).then_inc(sems["cc"], 1))
    wait("gpsimd", "cc", cc2_tgt)
    d_c2b = dma("gpsimd", "dG", ROWS[0:1, 0:128], cc2_out.ap())
    wait("vector", "dG", d_c2b)
    v_s1 = vop(lambda e, o=ROWS[0:1, 131:132], i_=ROWS[0:1, 0:128]:
               e.tensor_reduce(o, i_, axis=AX, op=ALU.max))
    wait("vector", "vS", v_s1)
    for fn_ in (
        lambda e: e.tensor_scalar(ROWS[0:1, 131:132], ROWS[0:1, 131:132], EPS_SCALE, None, ALU.max),
        lambda e: e.reciprocal(ROWS[0:1, 132:133], ROWS[0:1, 131:132]),
        lambda e: e.tensor_scalar(ROWS[0:1, 133:134], ROWS[0:1, 132:133], 127.0, None, ALU.mult),
        lambda e: e.tensor_tensor(ROWS[0:1, 134:135], ROWS[0:1, 131:132], ROWS[0:1, 129:130], ALU.mult),
        lambda e: e.tensor_scalar(ROWS[0:1, 134:135], ROWS[0:1, 134:135], 1.0 / 127.0, None, ALU.mult),
        lambda e: e.tensor_scalar(ROWS[0:1, 140:268], ONESF[0:1, :], ROWS[0:1, 133:134], None, ALU.mult),
        lambda e: e.tensor_scalar(ROWS[0:1, 268:396], ONESF[0:1, :], ROWS[0:1, 134:135], None, ALU.mult),
    ):
        t_ = vop(fn_)
        wait("vector", "vS", t_)
    v_s7 = t_
    wait("sync", "vS", v_s7)
    dma("sync", "dZ", COLS[:, 4:5], ROWS[0:1, 140:268])
    d_sc3 = dma("sync", "dZ", COLS[:, 5:6], ROWS[0:1, 268:396])
    wait("vector", "dZ", d_sc3)
    b_done = 0
    for t in range(TB):
        b_done = vop(lambda e, o=STA[:, t * 8 + 6:t * 8 + 7], a=STA[:, t * 8 + 4:t * 8 + 5],
                     b=COLS[:, 4:5]: e.tensor_tensor(o, a, b, ALU.mult), inc=(t == TB - 1))

    wait("vector", "tS", wo_mm_all)
    wait("tensor", "vS", max(b_done, tern_all))
    ntok_free = [0, 0]
    nfm_free = [0, 0, 0]
    ffn_bank_free = [0] * 8
    tr_evac = {}
    last_store = 0
    for t in range(TB):
        tok_sl = t % 2
        fm_sl = t % 3
        tok_ap = S3[:, tok_sl * 2048:(tok_sl + 1) * 2048]
        fm_base = 4096 + fm_sl * 2048
        wait("vector", "tS", ntok_free[tok_sl])
        vop(lambda e, o=WFT[:, 0:2048], i_=RREG[:, t * C:(t + 1) * C],
            b=STA[:, t * 8 + 6:t * 8 + 7], g=GAM[:]:
            e.scalar_tensor_tensor(o, i_, b, g, ALU.mult, ALU.mult), inc=False)
        vop(lambda e, o=WFT[:, 0:2048], i_=WFT[:, 0:2048]:
            e.tensor_scalar(o, i_, -128.0, 127.0, ALU.max, ALU.min), inc=False)
        v_q = vop(lambda e, o=tok_ap, i_=WFT[:, 0:2048]:
                  e.tensor_scalar(o, i_, MAGIC, -MAGIC, ALU.add, ALU.add))
        wait("tensor", "vS", v_q)
        for c in range(CB):
            bank = c % 2
            off = ((c // 2) % 4) * 128
            if (t, c - 2) in tr_evac:
                wait("tensor", "sS", tr_evac[(t, c - 2)])
            elif c < 2 and t > 0 and (t - 1, 14 + c) in tr_evac:
                wait("tensor", "sS", tr_evac[(t - 1, 14 + c)])
            tr = top(lambda e, o=PT[bank][:, off:off + 128],
                     i_=tok_ap[:, c * 128:(c + 1) * 128], iden=IDB[:, 0:128]:
                     e.transpose(o, i_, iden))
            wait("scalar", "tS", tr)
            tr_evac[(t, c)] = sop(lambda e, o=S3[:, fm_base + c * 128: fm_base + (c + 1) * 128],
                                  p=PT[bank][:, off:off + 128]: e.copy(out=o, in_=p))
        ntok_free[tok_sl] = P.cur("tS")
        trans_s = tr_evac[(t, CB - 1)]
        wait("tensor", "sS", trans_s)
        pb = 0
        for osl in range(4):
            wait("tensor", "vS", ffn_bank_free[pb + osl])
        for c in range(CB):
            lhs = S3[:, fm_base + c * 128: fm_base + (c + 1) * 128]
            for osl in range(4):
                zone = S2 if osl < 2 else S1
                rhs = zone[:, c * 1024 + (osl % 2) * 512: c * 1024 + (osl % 2) * 512 + 512]
                top(lambda e, o=PS[pb + osl][:], l=lhs, r=rhs, st=(c == 0), sp=(c == CB - 1):
                    e.matmul(o, l, r, start=st, stop=sp),
                    inc=(c == CB - 1 and osl == 3))
        ffn_done = P.cur("tS")
        nfm_free[fm_sl] = ffn_done
        wait("vector", "tS", ffn_done)
        for osl in range(4):
            ev = vop(lambda e, o=RREG[:, t * C + osl * 512: t * C + (osl + 1) * 512],
                     p=PS[pb + osl][:], sc=COLS[:, 5:6]:
                     e.scalar_tensor_tensor(o, p, sc, o, ALU.mult, ALU.add))
            ffn_bank_free[pb + osl] = ev
        for g in range(4):
            vop(lambda e, o=BNS[:, g * 6:(g + 1) * 6],
                i_=RREG[:, t * C + g * 512: t * C + (g + 1) * 512]:
                e.bn_stats(o, i_), inc=(g == 3))
        bn2 = P.cur("vS")
        wait("vector", "vS", bn2)
        t_ = vop(lambda e, o=STA[:, t * 8 + 6:t * 8 + 8], i_=BNS[:, 0:24]: e.bn_aggr(o, i_))
        wait("vector", "vS", t_)
        v_u2 = vop(lambda e, o=STA[:, t * 8 + 7:t * 8 + 8], i_=STA[:, t * 8 + 7:t * 8 + 8]:
                   e.tensor_scalar(o, i_, EPS_LN, None, ALU.add))
        wait("scalar", "vS", v_u2)
        s_sq2 = sop(lambda e, o=STA[:, t * 8 + 7:t * 8 + 8], i_=STA[:, t * 8 + 7:t * 8 + 8]:
                    e.activation(out=o, in_=i_, func=AF.Sqrt))
        wait("vector", "sS", s_sq2)
        t_ = vop(lambda e, o=STA[:, t * 8 + 7:t * 8 + 8], i_=STA[:, t * 8 + 7:t * 8 + 8]:
                 e.reciprocal(o, i_))
        wait("vector", "vS", t_)
        v_out = vop(lambda e, o=RREG[:, t * C:(t + 1) * C], i_=RREG[:, t * C:(t + 1) * C],
                    m=STA[:, t * 8 + 6:t * 8 + 7], s2=STA[:, t * 8 + 7:t * 8 + 8]:
                    e.tensor_scalar(o, i_, m, s2, ALU.subtract, ALU.mult))
        wait("sync", "vS", v_out)
        last_store = dma("sync", "dS", y_d[t * 128:(t + 1) * 128, :], RREG[:, t * C:(t + 1) * C])

    wait("sync", "dS", last_store)
    wait("gpsimd", "dS", last_store)

    with es:
        block = es.enter_context(nc.Block())

        @block.sync
        def _(e):
            for f in P.ins["sync"]:
                f(e)

        @block.gpsimd
        def _(e):
            for f in P.ins["gpsimd"]:
                f(e)

        @block.tensor
        def _(e):
            for f in P.ins["tensor"]:
                f(e)

        @block.vector
        def _(e):
            for f in P.ins["vector"]:
                f(e)

        @block.scalar
        def _(e):
            for f in P.ins["scalar"]:
                f(e)

    return nc


def kernel(L0_attn_x, L0_attn_cos, L0_attn_sin, L0_attn_Wq, L0_attn_Wk, L0_attn_Wv,
           L0_attn_Wo, L0_ffn_weight, L0_ffn_norm_weight):
    x = np.ascontiguousarray(np.asarray(L0_attn_x, dtype=np.float32)[0])
    xT = np.ascontiguousarray(x.T)
    wq_t = np.ascontiguousarray(np.asarray(L0_attn_Wq, dtype=np.float32).T)
    wk_t = np.ascontiguousarray(np.asarray(L0_attn_Wk, dtype=np.float32).T)
    wv_t = np.ascontiguousarray(np.asarray(L0_attn_Wv, dtype=np.float32).T)
    wo_t = np.ascontiguousarray(np.asarray(L0_attn_Wo, dtype=np.float32).T)
    wf_t = np.ascontiguousarray(np.asarray(L0_ffn_weight, dtype=np.float32).T)
    gam = np.ascontiguousarray(np.broadcast_to(
        np.asarray(L0_ffn_norm_weight, dtype=np.float32)[None, :], (128, C)))
    iden = np.eye(128, dtype=np.float32)
    ones = np.ones((128, 128), dtype=np.float32)

    nc = build_nc()
    in_maps = []
    for cid in range(NCORES):
        in_maps.append({
            "x_fm": np.ascontiguousarray(xT[:, cid * T:(cid + 1) * T]),
            "x_tok": np.ascontiguousarray(x[cid * T:(cid + 1) * T, :]),
            "wq_t": wq_t, "wk_t": wk_t, "wv_t": wv_t, "wo_t": wo_t, "wf_t": wf_t,
            "gamma_bc": gam, "id_bf": iden, "ones_f": ones,
        })
    res = run_bass_kernel_spmd(nc, in_maps, core_ids=list(range(NCORES)))
    out = np.concatenate([res.results[i]["y"] for i in range(NCORES)], axis=0)
    return out.reshape(1, 8192, C)
